# version 02
# baseline (speedup 1.0000x reference)
import numpy as np
import concourse.bass as bass
import concourse.mybir as mybir
from concourse.bass_utils import run_bass_kernel_spmd

F32 = mybir.dt.float32
BF16 = mybir.dt.bfloat16
AF = mybir.ActivationFunctionType
ALU = mybir.AluOpType
AX = mybir.AxisListType

NCORES = 8
T = 4096
L = 2048
NB = 2
D = 1024
E = 8
DI = 2048
DS = 16
DTR = 64
TC = 512
NTC = T // TC
DCH = D // 128
CHT = DI // 128
LC = 512
NLC = L // LC
EPS = 1e-5

_DEBUG = False


class Ctr:
    def __init__(self, nc, name):
        self.h = nc.alloc_semaphore(name)
        self.n = 0


class Plan:
    ENGS = ("sync", "tensor", "vector", "scalar", "gpsimd")

    def __init__(self):
        self.ops = {e: [] for e in self.ENGS}
        self.last_wait = {}

    def op(self, eng, fn):
        self.ops[eng].append(fn)

    def op_inc(self, eng, ctr, k, fn):
        def g(e, fn=fn, h=ctr.h, k=k):
            fn(e).then_inc(h, k)
        self.ops[eng].append(g)
        ctr.n += k
        return ctr.n

    def wait(self, eng, ctr, val):
        if val is None or val <= 0:
            return
        key = (eng, id(ctr))
        if self.last_wait.get(key, -1) >= val:
            return
        self.last_wait[key] = val
        self.ops[eng].append(lambda e, h=ctr.h, v=val: e.wait_ge(h, v))

    def dma(self, eng, ctr, out, in_):
        return self.op_inc(eng, ctr, 16,
                           lambda e, o=out, i=in_: e.dma_start(out=o, in_=i))

    def flush(self, nc):
        with nc.Block() as blk:
            for eng in self.ENGS:
                ops = self.ops[eng]
                if ops:
                    getattr(blk, eng)(
                        lambda e, ops=tuple(ops): [f(e) for f in ops])
        self.ops = {e: [] for e in self.ENGS}
        self.last_wait = {}


def build_program(upto=99):
    nc = bass.Bass("TRN2", target_bir_lowering=False, debug=False)

    def din(name, shape):
        return nc.dram_tensor(name, list(shape), F32, kind="ExternalInput").ap()

    X = din("X", (T, D))
    GW = din("GW", (D, E))
    GB = din("GB", (E, 1))
    W1 = din("W1", (E, D, D))
    W2 = din("W2", (E, D, D))
    W3 = din("W3", (E, D, D))
    B1 = din("B1", (E, 128, 8))
    B2 = din("B2", (E, 128, 8))
    B3 = din("B3", (E, D))
    NW = din("NW", (128, DCH))
    WIT = din("WIT", (D, 2 * DI))
    CW = din("CW", (128, CHT, 4))
    CB = din("CB", (128, CHT))
    XPT = din("XPT", (DI, 96))
    DPT = din("DPT", (DTR, DI))
    DTB = din("DTB", (128, CHT))
    ALOG = din("ALOG", (128, CHT * DS))
    DDI = din("DDI", (128, CHT))
    WOT = din("WOT", (DI, D))
    IDT = din("IDT", (128, 128))
    ONE1 = din("ONE1", (1, 128))
    ONEC = din("ONEC", (128, 1))
    SEL = din("SEL", (32, 32 * 128))

    XOUT = nc.dram_tensor("XOUT", [T, D], F32, kind="ExternalOutput").ap()
    skind = "ExternalOutput" if _DEBUG else "Internal"

    def dsc(name, shape, dt=F32):
        return nc.dram_tensor(name, list(shape), dt, kind=skind).ap()

    X1 = dsc("X1", (T, D))
    X2 = dsc("X2", (T, D))
    LNTM = dsc("LNTM", (T, D))
    AFH = dsc("AFH", (DCH, 128, T), BF16)
    AFL = dsc("AFL", (DCH, 128, T), BF16)
    XZPRE = dsc("XZPRE", (CHT, 128, T))
    SILUZ = dsc("SILUZ", (CHT, 128, T))
    XIACT = dsc("XIACT", (CHT, 128, T))
    DTPRE = dsc("DTPRE", (CHT, 128, T))
    YCH = dsc("YCH", (CHT, 128, T))
    CCIN = nc.dram_tensor("CCIN", [E, D], F32)
    CCOUT = nc.dram_tensor("CCOUT", [E, D], F32)
    if _DEBUG:
        DBG_SLAB = nc.dram_tensor("DBG_SLAB", [128, 2 * T], F32,
                                  kind="ExternalOutput").ap()
        DBG_V = nc.dram_tensor("DBG_V", [E, D], F32, kind="ExternalOutput").ap()
        DBG_TM = nc.dram_tensor("DBG_TM", [128, 3 * 256], F32,
                                kind="ExternalOutput").ap()

    def sb(name, shape, dt=F32):
        return nc.alloc_sbuf_tensor(name, list(shape), dt)

    idt = sb("idt", (128, 128))
    one1 = sb("one1", (1, 128))
    onec = sb("onec", (128, 1))
    gw = sb("gw", (128, DCH * E))
    gb = sb("gb", (E, 1))
    b1c = sb("b1c", (128, E * 8))
    b2c = sb("b2c", (128, E * 8))
    nw = sb("nw", (128, DCH))
    cw = sb("cw", (128, CHT * 4))
    cb = sb("cb", (128, CHT))
    dtb = sb("dtb", (128, CHT))
    aneg = sb("aneg", (128, CHT * DS))
    ddi = sb("ddi", (128, CHT))
    ptm = sb("ptm", (128, 32 * E))
    indtm = sb("indtm", (128, 32 * E))
    wtk = sb("wtk", (128, 32 * E))
    colbuf = sb("colbuf", (128, 16))
    msk_sb = sb("msk_sb", (128, 8))
    col8 = sb("col8", (E, 4))
    epscol = sb("epscol", (128, 1))
    wcol = sb("wcol", (128, 2))
    slab = sb("slab", (128, 2 * T))
    logits = slab[0:E, 0:T]
    probs = logits
    ind_em = slab[0:E, T:2 * T]
    bc = slab[0:32, 0:T]
    sel = slab[0:32, T:2 * T]

    AR_W = 43300
    arena = sb("arena", (128, AR_W))

    class Arena:
        def __init__(self):
            self.off = 0
            self.marks = []

        def f32(self, ncols):
            a = self.off
            self.off += ncols
            assert self.off <= AR_W, f"arena overflow {self.off}"
            return arena[:, a:a + ncols]

        def bf16(self, ncols):
            assert ncols % 2 == 0
            return self.f32(ncols // 2).bitcast(BF16)

        def push(self):
            self.marks.append(self.off)

        def pop(self):
            self.off = self.marks.pop()

    ar = Arena()

    P = [nc.alloc_psum_tensor(f"pp{i}", [128, 512], F32) for i in range(6)]
    PV = nc.alloc_psum_tensor("pv", [1, 1024], F32)

    SQ = Ctr(nc, "SQ")
    SG = Ctr(nc, "SG")
    ST = Ctr(nc, "ST")
    SV = Ctr(nc, "SV")
    SS = Ctr(nc, "SS")
    SC = Ctr(nc, "SC")
    SP = Ctr(nc, "SP")
    SPH = Ctr(nc, "SPH")

    plan = Plan()

    psum_state = {i: (None, 0) for i in range(6)}
    psum_ready = {i: 0 for i in range(6)}
    pv_state = [(None, 0)]

    def mm_bank(i):
        c, v = psum_state[i]
        if c is not None:
            plan.wait("tensor", c, v)

    def bank_done(i):
        psum_ready[i] = ST.n

    def evac_wait(eng, i):
        plan.wait(eng, ST, psum_ready[i])

    def evac_done(eng, i):
        ctr = {"vector": SV, "scalar": SS}[eng]
        psum_state[i] = (ctr, ctr.n)

    def barrier():
        for eng in Plan.ENGS:
            plan.op_inc(eng, SPH, 1, lambda e: e.nop())
        tgt = SPH.n
        for eng in Plan.ENGS:
            plan.wait(eng, SPH, tgt)

    def emit_const_loads():
        plan.dma("sync", SQ, idt[:], IDT)
        plan.dma("sync", SQ, one1[:], ONE1)
        plan.dma("sync", SQ, onec[:], ONEC)
        plan.dma("sync", SQ, gw[:].rearrange("p (c e) -> p c e", c=DCH),
                 GW.rearrange("(c p) e -> c p e", p=128).transpose([1, 0, 2]))
        plan.dma("sync", SQ, gb[:], GB)
        plan.dma("sync", SQ, b1c[:].rearrange("p (e o) -> p e o", e=E),
                 B1.transpose([1, 0, 2]))
        plan.dma("sync", SQ, b2c[:].rearrange("p (e o) -> p e o", e=E),
                 B2.transpose([1, 0, 2]))
        plan.dma("sync", SQ, nw[:], NW)
        plan.dma("sync", SQ, cw[:].rearrange("p (c j) -> p c j", c=CHT), CW)
        plan.dma("sync", SQ, cb[:], CB)
        plan.dma("sync", SQ, dtb[:], DTB)
        plan.dma("sync", SQ, aneg[:], ALOG)
        plan.dma("sync", SQ, ddi[:], DDI)
        ld = SQ.n
        plan.op("vector", lambda e: e.memset(epscol[:], EPS))
        plan.op("scalar", lambda e: e.memzero(wcol[:, 0:1]))
        plan.op("scalar", lambda e: e.drain())
        plan.op("scalar", lambda e: e.activation(
            out=wcol[:, 1:2], in_=wcol[:, 0:1], func=AF.Exp))
        plan.op("scalar", lambda e: e.nop(cycle_cnt=8000))
        plan.wait("scalar", SQ, ld)
        plan.op_inc("scalar", SS, 1,
                    lambda e: e.activation(out=aneg[:], in_=aneg[:], func=AF.Exp))
        plan.wait("vector", SS, SS.n)
        plan.op("vector", lambda e: e.tensor_scalar(out=aneg[:], in0=aneg[:],
                                                    scalar1=-1.0, scalar2=None,
                                                    op0=ALU.mult))
        plan.op_inc("vector", SV, 1, lambda e: e.drain())
        plan.wait("scalar", SV, SV.n)

    def emit_ln_chunk(xin, lnbuf, rms, lnsq):
        m = colbuf[:, 0:1]
        s2 = colbuf[:, 1:2]
        s = colbuf[:, 2:3]
        rr = colbuf[:, 3:4]
        nms = colbuf[:, 6:7]
        for tt in range(4):
            xt = xin[:, tt * D:(tt + 1) * D]
            lt = lnbuf[:, tt * D:(tt + 1) * D]
            plan.wait("vector", SS, SS.n)
            plan.op("vector", lambda e, xt=xt: e.tensor_tensor(
                out=lnsq, in0=xt, in1=xt, op=ALU.mult))
            plan.op("vector", lambda e: e.memset(m, 0.0))
            plan.op("vector", lambda e: e.drain())
            plan.op("vector", lambda e, xt=xt: e.tensor_reduce(
                out=m, in_=xt, axis=AX.X, op=ALU.add))
            plan.op("vector", lambda e: e.memset(s2, 0.0))
            plan.op("vector", lambda e: e.drain())
            plan.op("vector", lambda e: e.tensor_reduce(
                out=s2, in_=lnsq, axis=AX.X, op=ALU.add))
            plan.op("vector", lambda e: e.drain())
            plan.op("vector", lambda e: e.tensor_scalar(
                out=m, in0=m, scalar1=1.0 / D, scalar2=None, op0=ALU.mult))
            plan.op("vector", lambda e: e.tensor_scalar(
                out=s2, in0=s2, scalar1=1.0 / D, scalar2=None, op0=ALU.mult))
            plan.op("vector", lambda e: e.drain())
            plan.op("vector", lambda e: e.tensor_scalar(
                out=rr, in0=m, scalar1=m, scalar2=None, op0=ALU.mult))
            plan.op("vector", lambda e: e.drain())
            plan.op("vector", lambda e: e.tensor_scalar(
                out=s2, in0=s2, scalar1=rr, scalar2=None, op0=ALU.subtract))
            plan.op_inc("vector", SV, 1, lambda e: e.drain())
            var_done = SV.n
            plan.wait("scalar", SV, var_done)
            plan.op("scalar", lambda e: e.drain())
            plan.op("scalar", lambda e: e.activation(
                out=s, in_=s2, func=AF.Ln, bias=epscol[:, 0:1]))
            plan.op("scalar", lambda e: e.activation(
                out=s, in_=s, func=AF.Exp, scale=-0.5))
            plan.op("scalar", lambda e: e.drain())
            plan.op("scalar", lambda e: e.activation(
                out=nms, in_=s, func=AF.Identity, scale=m))
            plan.op("scalar", lambda e: e.activation(
                out=nms, in_=nms, func=AF.Identity, scale=-1.0))
            plan.op("scalar", lambda e: e.drain())
            plan.op_inc("scalar", SS, 1, lambda e, xt=xt, lt=lt: e.activation(
                out=lt, in_=xt, func=AF.Identity, scale=s, bias=nms))
            if rms:
                plan.wait("vector", SS, SS.n)
                plan.op("vector", lambda e, lt=lt, lnsq=lnsq: e.tensor_tensor(
                    out=lnsq, in0=lt, in1=lt, op=ALU.mult))
                plan.op("vector", lambda e: e.memset(rr, 0.0))
                plan.op("vector", lambda e: e.drain())
                plan.op("vector", lambda e: e.tensor_reduce(
                    out=rr, in_=lnsq, axis=AX.X, op=ALU.add))
                plan.op("vector", lambda e: e.drain())
                plan.op("vector", lambda e: e.tensor_scalar(
                    out=rr, in0=rr, scalar1=1.0 / D, scalar2=None, op0=ALU.mult))
                plan.op_inc("vector", SV, 1, lambda e: e.drain())
                plan.wait("scalar", SV, SV.n)
                plan.op("scalar", lambda e: e.drain())
                plan.op("scalar", lambda e: e.activation(
                    out=rr, in_=rr, func=AF.Ln, bias=epscol[:, 0:1]))
                plan.op("scalar", lambda e: e.activation(
                    out=rr, in_=rr, func=AF.Exp, scale=-0.5))
                plan.op("scalar", lambda e: e.drain())
                plan.op_inc("scalar", SS, 1, lambda e, lt=lt: e.activation(
                    out=lt, in_=lt, func=AF.Identity, scale=rr))

    def emit_norm_phase(src, rms, gate, store_lntm, nw_scale):
        ar.push()
        xin = [ar.f32(4 * D) for _ in range(2)]
        lnbuf = ar.f32(4 * D)
        lnsq = ar.f32(D)
        fmf = ar.f32(DCH * TC)
        fmh = ar.bf16(DCH * TC)
        fml = ar.bf16(DCH * TC)
        xin_free = [(None, 0), (None, 0)]
        lnbuf_free_t = 0
        lnbuf_free_q = 0
        fm_store_done = 0
        gate_done_t = 0
        for tc in range(NTC):
            sl = slice(tc * TC, (tc + 1) * TC)
            buf = xin[tc % 2]
            c, v = xin_free[tc % 2]
            if c is not None:
                plan.wait("sync", c, v)
            ld = plan.dma("sync", SQ, buf.rearrange("p (t d) -> p t d", t=4),
                          src[sl, :].rearrange("(t p) d -> t p d", p=128)
                          .transpose([1, 0, 2]))
            plan.wait("vector", SQ, ld)
            plan.wait("scalar", SQ, ld)
            plan.wait("vector", ST, lnbuf_free_t)
            plan.wait("vector", SQ, lnbuf_free_q)
            emit_ln_chunk(buf, lnbuf, rms, lnsq)
            ln_done = SS.n
            xin_free[tc % 2] = (SS, ln_done)
            if store_lntm:
                plan.wait("sync", SS, ln_done)
                lnbuf_free_q = plan.dma(
                    "sync", SQ,
                    LNTM[sl, :].rearrange("(t p) d -> t p d", p=128)
                    .transpose([1, 0, 2]),
                    lnbuf.rearrange("p (t d) -> p t d", t=4))
            plan.wait("tensor", SS, ln_done)
            for tt in range(4):
                for half in range(2):
                    b = 4 + half
                    mm_bank(b)
                    for j in range(4):
                        dc = half * 4 + j
                        plan.op_inc(
                            "tensor", ST, 1,
                            lambda e, tt=tt, dc=dc, b=b, j=j: e.transpose(
                                P[b][:, j * 128:(j + 1) * 128],
                                lnbuf[:, tt * D + dc * 128:
                                      tt * D + (dc + 1) * 128],
                                idt[:]))
                    bank_done(b)
                    evac_wait("scalar", b)
                    plan.wait("scalar", SQ, fm_store_done)
                    plan.wait("scalar", ST, gate_done_t)

                    def ev(e, tt=tt, half=half, b=b):
                        return e.activation(
                            out=fmf.rearrange("p (c t) -> p c t", c=DCH)
                            [:, half * 4:(half + 1) * 4,
                             tt * 128:(tt + 1) * 128],
                            in_=P[b][:].rearrange("p (c t) -> p c t", c=4),
                            func=AF.Identity)
                    plan.op_inc("scalar", SS, 1, ev)
                    evac_done("scalar", b)
            lnbuf_free_t = ST.n
            fm_done_c, fm_done_v = SS, SS.n
            if nw_scale:
                plan.wait("vector", SS, SS.n)
                for dc in range(DCH):
                    plan.op_inc("vector", SV, 1, lambda e, dc=dc: e.tensor_scalar(
                        out=fmf[:, dc * TC:(dc + 1) * TC],
                        in0=fmf[:, dc * TC:(dc + 1) * TC],
                        scalar1=nw[:, dc:dc + 1], scalar2=None, op0=ALU.mult))
                fm_done_c, fm_done_v = SV, SV.n
                plan.wait("scalar", SV, SV.n)
            plan.op_inc("scalar", SS, 1,
                        lambda e: e.activation(out=fmh, in_=fmf,
                                               func=AF.Identity))
            hi_done = SS.n
            plan.wait("vector", SS, hi_done)
            plan.wait("vector", SQ, fm_store_done)
            plan.op_inc("vector", SV, 1, lambda e: e.tensor_tensor(
                out=fml, in0=fmf, in1=fmh, op=ALU.subtract))
            lo_done = SV.n
            plan.wait("sync", SS, hi_done)
            plan.wait("sync", SV, lo_done)
            plan.dma("sync", SQ, AFH[:, :, sl].transpose([1, 0, 2]),
                     fmh.rearrange("p (c t) -> p c t", c=DCH))
            fm_store_done = plan.dma(
                "sync", SQ, AFL[:, :, sl].transpose([1, 0, 2]),
                fml.rearrange("p (c t) -> p c t", c=DCH))
            if gate:
                mm_bank(0)
                plan.wait("tensor", fm_done_c, fm_done_v)
                for dc in range(DCH):
                    plan.op_inc("tensor", ST, 1, lambda e, dc=dc, sl=sl: e.matmul(
                        P[0][0:E, :], gw[:, dc * E:(dc + 1) * E],
                        fmf[:, dc * TC:(dc + 1) * TC],
                        start=(dc == 0), stop=(dc == DCH - 1)))
                gate_done_t = ST.n
                bank_done(0)
                evac_wait("vector", 0)
                plan.op_inc("vector", SV, 1, lambda e, sl=sl: e.tensor_scalar(
                    out=logits[:, sl], in0=P[0][0:E, :], scalar1=gb[:, 0:1],
                    scalar2=None, op0=ALU.add))
                evac_done("vector", 0)
        ar.pop()

    def emit_gate_post():
        mx = col8[:, 0:1]
        sm = col8[:, 1:2]
        for b in range(NB):
            sl = slice(b * L, (b + 1) * L)
            plan.op("vector", lambda e: e.memset(mx, -3e38))
            plan.op("vector", lambda e: e.drain())
            plan.op("vector", lambda e, sl=sl: e.tensor_reduce(
                out=mx, in_=logits[:, sl], axis=AX.X, op=ALU.max))
            plan.op("vector", lambda e: e.drain())
            plan.op("vector", lambda e: e.tensor_scalar(
                out=mx, in0=mx, scalar1=-1.0, scalar2=None, op0=ALU.mult))
            plan.op_inc("vector", SV, 1, lambda e: e.drain())
            plan.wait("scalar", SV, SV.n)
            plan.op("scalar", lambda e: e.drain())
            plan.op("scalar", lambda e, sl=sl: e.activation(
                out=probs[:, sl], in_=logits[:, sl], func=AF.Exp, bias=mx,
                accum_out=sm))
            plan.op("scalar", lambda e: e.activation(
                out=col8[:, 2:3], in_=sm, func=AF.Identity))
            plan.op_inc("scalar", SS, 1, lambda e: e.drain())
            plan.wait("vector", SS, SS.n)
            plan.op("vector", lambda e: e.drain())
            plan.op("vector", lambda e: e.reciprocal(out=col8[:, 3:4], in_=sm))
            plan.op("vector", lambda e: e.drain())
            plan.op_inc("vector", SV, 1, lambda e, sl=sl: e.tensor_scalar(
                out=probs[:, sl], in0=probs[:, sl], scalar1=col8[:, 3:4],
                scalar2=None, op0=ALU.mult))
        probs_done = SV.n
        plan.wait("tensor", SV, probs_done)
        mm_bank(4)
        for i in range(32):
            plan.op_inc("tensor", ST, 1, lambda e, i=i: e.transpose(
                P[4][:, i * 8:(i + 1) * 8],
                probs[:, i * 128:(i + 1) * 128], idt[0:E, 0:E]))
        bank_done(4)
        evac_wait("vector", 4)
        plan.op_inc("vector", SV, 1,
                    lambda e: e.tensor_copy(out=ptm[:], in_=P[4][:, 0:256]))
        evac_done("vector", 4)
        m1 = colbuf[:, 4:5]
        m2 = colbuf[:, 5:6]
        msk = msk_sb[:]
        for i in range(32):
            pp = ptm[:, i * 8:(i + 1) * 8]
            it = indtm[:, i * 8:(i + 1) * 8]
            wt = wtk[:, i * 8:(i + 1) * 8]
            plan.op("vector", lambda e: e.memset(m1, -3e38))
            plan.op("vector", lambda e: e.drain())
            plan.op("vector", lambda e, pp=pp: e.tensor_reduce(
                out=m1, in_=pp, axis=AX.X, op=ALU.max))
            plan.op("vector", lambda e: e.drain())
            plan.op("vector", lambda e, pp=pp: e.tensor_scalar(
                out=msk, in0=pp, scalar1=m1, scalar2=-1e30,
                op0=ALU.is_ge, op1=ALU.mult))
            plan.op("vector", lambda e: e.drain())
            plan.op("vector", lambda e, pp=pp: e.tensor_tensor(
                out=msk, in0=msk, in1=pp, op=ALU.add))
            plan.op("vector", lambda e: e.memset(m2, -3e38))
            plan.op("vector", lambda e: e.drain())
            plan.op("vector", lambda e: e.tensor_reduce(
                out=m2, in_=msk, axis=AX.X, op=ALU.max))
            plan.op("vector", lambda e: e.drain())
            plan.op("vector", lambda e, pp=pp, it=it: e.tensor_scalar(
                out=it, in0=pp, scalar1=m2, scalar2=None, op0=ALU.is_ge))
            plan.op("vector", lambda e: e.drain())
            plan.op_inc("vector", SV, 1, lambda e, pp=pp, it=it, wt=wt:
                        e.tensor_tensor(out=wt, in0=pp, in1=it, op=ALU.mult))
        top2_done = SV.n
        plan.wait("tensor", SV, top2_done)
        for g in range(8):
            b = 4 + (g % 2)
            mm_bank(b)
            for j in range(4):
                i = g * 4 + j
                plan.op_inc("tensor", ST, 1, lambda e, i=i, j=j, b=b: e.transpose(
                    P[b][0:E, j * 128:(j + 1) * 128],
                    indtm[:, i * 8:(i + 1) * 8], idt[:]))
            bank_done(b)
            evac_wait("vector", b)
            plan.op_inc("vector", SV, 1, lambda e, g=g, b=b: e.tensor_copy(
                out=ind_em[:, g * 512:(g + 1) * 512], in_=P[b][0:E, :]))
            evac_done("vector", b)

    def emit_ffn_experts(vstage, b3row):
        ar.push()
        w1h = ar.bf16(DCH * D)
        w1l = ar.bf16(DCH * D)
        w2h = ar.bf16(DCH * D)
        w2l = ar.bf16(DCH * D)
        w3h = ar.bf16(DCH * D)
        w3l = ar.bf16(DCH * D)
        wstage = [ar.f32(512) for _ in range(2)]
        lnh = [ar.bf16(DCH * TC)]
        lnl = [ar.bf16(DCH * TC)]
        h1h = ar.bf16(DCH * TC)
        h1l = ar.bf16(DCH * TC)
        h2h = ar.bf16(DCH * TC)
        h2l = ar.bf16(DCH * TC)
        ev32 = [ar.f32(TC) for _ in range(2)]
        wy = [ar.f32(TC) for _ in range(2)]

        wmats = {0: (W1, w1h, w1l), 1: (W2, w2h, w2l), 2: (W3, w3h, w3l)}
        stage_free = [(None, 0), (None, 0)]
        ln_free = [(None, 0)]
        vstage_free = [(None, 0)]
        ev_free = [(None, 0), (None, 0)]
        wy_free = [(None, 0), (None, 0)]
        pend_pv = [None]

        mat_free = [0, 0, 0]
        for e_i in range(E):
            prev_mm = ST.n
            if e_i > 0:
                plan.wait("sync", ST, mat_free[0])
            for mi in range(3):
                WD, wh, wl = wmats[mi]
                if e_i > 0:
                    plan.wait("scalar", ST, mat_free[mi])
                    plan.wait("vector", ST, mat_free[mi])
                    plan.wait("sync", ST, mat_free[mi])
                for kc in range(DCH):
                    for q in range(2):
                        si = (mi * DCH * 2 + kc * 2 + q) % 2
                        c, v = stage_free[si]
                        if c is not None:
                            plan.wait("sync", c, v)
                        ld = plan.dma("sync", SQ, wstage[si],
                                      WD[e_i, kc * 128:(kc + 1) * 128,
                                         q * 512:(q + 1) * 512])
                        plan.wait("scalar", SQ, ld)
                        co = kc * D + q * 512
                        plan.op_inc("scalar", SS, 1,
                                    lambda e, si=si, wh=wh, co=co: e.activation(
                                        out=wh[:, co:co + 512],
                                        in_=wstage[si], func=AF.Identity))
                        plan.wait("vector", SS, SS.n)
                        sf = plan.op_inc("vector", SV, 1,
                                         lambda e, si=si, wh=wh, wl=wl, co=co:
                                         e.tensor_tensor(
                                             out=wl[:, co:co + 512],
                                             in0=wstage[si],
                                             in1=wh[:, co:co + 512],
                                             op=ALU.subtract))
                        stage_free[si] = (SV, sf)
            wsplit_v = SV.n
            wsplit_s = SS.n
            plan.wait("sync", ST, prev_mm)
            ldb3 = plan.dma("sync", SQ, b3row, B3[e_i:e_i + 1, :])
            plan.wait("tensor", SV, wsplit_v)
            plan.wait("tensor", SS, wsplit_s)
            plan.wait("tensor", SQ, ldb3)

            def layer(wh, wl, xh, xl, outh, outl, bcol_of):
                for oc in range(DCH):
                    bk = oc % 2
                    mm_bank(bk)
                    for kc in range(DCH):
                        for p_i, (wa, xa) in enumerate(
                                ((wh, xh), (wh, xl), (wl, xh))):
                            plan.op_inc(
                                "tensor", ST, 1,
                                lambda e, wa=wa, xa=xa, oc=oc, kc=kc, bk=bk,
                                st=(kc == 0 and p_i == 0),
                                sp=(kc == DCH - 1 and p_i == 2): e.matmul(
                                    P[bk][:],
                                    wa[:, kc * D + oc * 128:
                                       kc * D + (oc + 1) * 128],
                                    xa[:, kc * TC:(kc + 1) * TC],
                                    start=st, stop=sp))
                    bank_done(bk)
                    evac_wait("scalar", bk)
                    ei = oc % 2
                    c, v = ev_free[ei]
                    if c is not None:
                        plan.wait("scalar", c, v)
                    bcol = bcol_of(oc)
                    plan.op_inc("scalar", SS, 1,
                                lambda e, bk=bk, ei=ei, bcol=bcol: e.activation(
                                    out=ev32[ei], in_=P[bk][:], func=AF.Relu,
                                    bias=bcol))
                    evac_done("scalar", bk)
                    plan.op_inc("scalar", SS, 1,
                                lambda e, ei=ei, oc=oc, outh=outh: e.activation(
                                    out=outh[:, oc * TC:(oc + 1) * TC],
                                    in_=ev32[ei], func=AF.Identity))
                    plan.wait("vector", SS, SS.n)
                    ef = plan.op_inc("vector", SV, 1,
                                     lambda e, ei=ei, oc=oc, outh=outh,
                                     outl=outl: e.tensor_tensor(
                                         out=outl[:, oc * TC:(oc + 1) * TC],
                                         in0=ev32[ei],
                                         in1=outh[:, oc * TC:(oc + 1) * TC],
                                         op=ALU.subtract))
                    ev_free[ei] = (SV, ef)

            for tc in range(NTC):
                si = 0
                c, v = ln_free[si]
                if c is not None:
                    plan.wait("sync", c, v)
                sl = slice(tc * TC, (tc + 1) * TC)
                plan.dma("sync", SQ, lnh[si].rearrange("p (c t) -> p c t", c=DCH),
                         AFH[:, :, sl].transpose([1, 0, 2]))
                lnld = plan.dma("sync", SQ,
                                lnl[si].rearrange("p (c t) -> p c t", c=DCH),
                                AFL[:, :, sl].transpose([1, 0, 2]))
                plan.wait("tensor", SQ, lnld)

                layer(w1h, w1l, lnh[si], lnl[si], h1h, h1l,
                      lambda oc: b1c[:, e_i * 8 + oc:e_i * 8 + oc + 1])
                ln_free[si] = (ST, ST.n)
                if tc == NTC - 1:
                    mat_free[0] = ST.n
                plan.wait("tensor", SV, SV.n)
                layer(w2h, w2l, h1h, h1l, h2h, h2l,
                      lambda oc: b2c[:, e_i * 8 + oc:e_i * 8 + oc + 1])
                if tc == NTC - 1:
                    mat_free[1] = ST.n
                plan.wait("tensor", SV, SV.n)
                for tt in range(4):
                    for oh in range(2):
                        bk = 4 + (tt * 2 + oh) % 2
                        mm_bank(bk)
                        if pend_pv[0] is not None:
                            pend_pv[0]()
                            pend_pv[0] = None
                        plan.op_inc("tensor", ST, 1,
                                    lambda e, bk=bk, oh=oh: e.matmul(
                                        P[bk][:], one1[:],
                                        b3row[:, oh * 512:(oh + 1) * 512],
                                        start=True, stop=False))
                        for kc in range(DCH):
                            for p_i, (xa, wa) in enumerate(
                                    ((h2h, w3h), (h2h, w3l), (h2l, w3h))):
                                plan.op_inc(
                                    "tensor", ST, 1,
                                    lambda e, xa=xa, wa=wa, kc=kc, tt=tt, oh=oh,
                                    bk=bk,
                                    sp=(kc == DCH - 1 and p_i == 2): e.matmul(
                                        P[bk][:],
                                        xa[:, kc * TC + tt * 128:
                                           kc * TC + (tt + 1) * 128],
                                        wa[:, kc * D + oh * 512:
                                           kc * D + (oh + 1) * 512],
                                        start=False, stop=sp))
                        bank_done(bk)
                        evac_wait("scalar", bk)
                        wi = (tt * 2 + oh) % 2
                        c, v = wy_free[wi]
                        if c is not None:
                            plan.wait("scalar", c, v)
                        gtile = tc * 4 + tt
                        plan.op_inc("scalar", SS, 1,
                                    lambda e, bk=bk, wi=wi, gtile=gtile, e_i=e_i:
                                    e.activation(
                                        out=wy[wi], in_=P[bk][:], func=AF.Relu,
                                        scale=wtk[:, gtile * 8 + e_i:
                                                  gtile * 8 + e_i + 1]))
                        evac_done("scalar", bk)
                        plan.wait("tensor", SS, SS.n)
                        cpv, vpv = pv_state[0]
                        if cpv is not None:
                            plan.wait("tensor", cpv, vpv)
                        wf = plan.op_inc(
                            "tensor", ST, 1,
                            lambda e, wi=wi, oh=oh,
                            st=(tc == 0 and tt == 0),
                            sp=(tc == NTC - 1 and tt == 3): e.matmul(
                                PV[:, oh * 512:(oh + 1) * 512], onec[:],
                                wy[wi], start=st, stop=sp))
                        wy_free[wi] = (ST, wf)
            if pend_pv[0] is not None:
                pend_pv[0]()
                pend_pv[0] = None
            v_mm_done = ST.n
            mat_free[2] = v_mm_done
            plan.wait("vector", ST, v_mm_done)
            c, v = vstage_free[0]
            if c is not None:
                plan.wait("vector", c, v)
            plan.op_inc("vector", SV, 1,
                        lambda e: e.tensor_copy(out=vstage, in_=PV[:]))
            vst = SV.n
            pv_state[0] = (SV, vst)
            plan.wait("sync", SV, vst)
            vf = plan.dma("sync", SQ, CCIN.ap()[e_i:e_i + 1, :], vstage)
            vstage_free[0] = (SQ, vf)
        ar.pop()

    def emit_allreduce(vall):
        plan.wait("gpsimd", SQ, SQ.n)
        plan.op_inc("gpsimd", SC, 1, lambda e: e.collective_compute(
            "AllReduce", ALU.add, replica_groups=[list(range(NCORES))],
            ins=[CCIN.ap().opt()], outs=[CCOUT.ap().opt()]))
        plan.wait("gpsimd", SC, SC.n)
        plan.dma("gpsimd", SG, vall, CCOUT.ap())

    def emit_ffn_apply(src, dst, vall):
        ar.push()
        xres = [ar.f32(4 * D) for _ in range(2)]
        xo = [ar.f32(512) for _ in range(2)]
        xres_free = [(None, 0), (None, 0)]
        xo_free = [(None, 0), (None, 0)]
        plan.wait("tensor", SG, SG.n)
        plan.wait("vector", SG, SG.n)
        for tc in range(NTC):
            si = tc % 2
            c, v = xres_free[si]
            if c is not None:
                plan.wait("sync", c, v)
            sl = slice(tc * TC, (tc + 1) * TC)
            xl = plan.dma("sync", SQ, xres[si].rearrange("p (t d) -> p t d", t=4),
                          src[sl, :].rearrange("(t p) d -> t p d", p=128)
                          .transpose([1, 0, 2]))
            plan.wait("vector", SQ, xl)
            for tt in range(4):
                for oh in range(2):
                    bk = (tt * 2 + oh) % 2
                    mm_bank(bk)
                    g = tc * 4 + tt
                    plan.op_inc("tensor", ST, 1, lambda e, g=g, oh=oh, bk=bk:
                                e.matmul(P[bk][:],
                                         ind_em[:, g * 128:(g + 1) * 128],
                                         vall[:, oh * 512:(oh + 1) * 512],
                                         start=True, stop=True))
                    bank_done(bk)
                    evac_wait("vector", bk)
                    oi = (tt * 2 + oh) % 2
                    c, v = xo_free[oi]
                    if c is not None:
                        plan.wait("vector", c, v)
                    plan.op_inc("vector", SV, 1,
                                lambda e, bk=bk, oi=oi, tt=tt, oh=oh, si=si:
                                e.scalar_tensor_tensor(
                                    out=xo[oi], in0=P[bk][:], scalar=0.5,
                                    in1=xres[si][:, tt * D + oh * 512:
                                                 tt * D + (oh + 1) * 512],
                                    op0=ALU.mult, op1=ALU.add))
                    evac_done("vector", bk)
                    plan.wait("sync", SV, SV.n)
                    of = plan.dma("sync", SQ,
                                  dst[tc * TC + tt * 128:
                                      tc * TC + (tt + 1) * 128,
                                      oh * 512:(oh + 1) * 512], xo[oi])
                    xo_free[oi] = (SQ, of)
            xres_free[si] = (SV, SV.n)
        ar.pop()

    def emit_inproj():
        ar.push()
        wih = ar.bf16(DCH * 2 * DI)
        wil = ar.bf16(DCH * 2 * DI)
        wst = [ar.f32(512) for _ in range(2)]
        hh = [ar.bf16(DCH * TC) for _ in range(2)]
        hl = [ar.bf16(DCH * TC) for _ in range(2)]
        ev = [ar.f32(TC) for _ in range(2)]
        stage_free = [(None, 0), (None, 0)]
        h_free = [(None, 0), (None, 0)]
        ev_free = [(None, 0)] * 2
        for kc in range(DCH):
            for q in range(8):
                si = (kc * 8 + q) % 2
                c, v = stage_free[si]
                if c is not None:
                    plan.wait("sync", c, v)
                ld = plan.dma("sync", SQ, wst[si],
                              WIT[kc * 128:(kc + 1) * 128,
                                  q * 512:(q + 1) * 512])
                plan.wait("scalar", SQ, ld)
                co = kc * 2 * DI + q * 512
                plan.op_inc("scalar", SS, 1, lambda e, si=si, co=co:
                            e.activation(out=wih[:, co:co + 512],
                                         in_=wst[si], func=AF.Identity))
                plan.wait("vector", SS, SS.n)
                sf = plan.op_inc("vector", SV, 1, lambda e, si=si, co=co:
                                 e.tensor_tensor(
                                     out=wil[:, co:co + 512],
                                     in0=wst[si],
                                     in1=wih[:, co:co + 512],
                                     op=ALU.subtract))
                stage_free[si] = (SV, sf)
        plan.wait("tensor", SV, SV.n)
        plan.wait("tensor", SS, SS.n)
        for tc in range(NTC):
            si = tc % 2
            c, v = h_free[si]
            if c is not None:
                plan.wait("sync", c, v)
            sl = slice(tc * TC, (tc + 1) * TC)
            plan.dma("sync", SQ, hh[si].rearrange("p (c t) -> p c t", c=DCH),
                     AFH[:, :, sl].transpose([1, 0, 2]))
            hld = plan.dma("sync", SQ,
                           hl[si].rearrange("p (c t) -> p c t", c=DCH),
                           AFL[:, :, sl].transpose([1, 0, 2]))
            plan.wait("tensor", SQ, hld)
            for rc in range(32):
                bk = rc % 4
                mm_bank(bk)
                for kc in range(DCH):
                    for p_i, (wa, xa) in enumerate(
                            ((wih, hh[si]), (wih, hl[si]), (wil, hh[si]))):
                        plan.op_inc(
                            "tensor", ST, 1,
                            lambda e, wa=wa, xa=xa, kc=kc, rc=rc, bk=bk,
                            st=(kc == 0 and p_i == 0),
                            sp=(kc == DCH - 1 and p_i == 2): e.matmul(
                                P[bk][:],
                                wa[:, kc * 2 * DI + rc * 128:
                                   kc * 2 * DI + (rc + 1) * 128],
                                xa[:, kc * TC:(kc + 1) * TC],
                                start=st, stop=sp))
                bank_done(bk)
                ei = rc % 2
                c, v = ev_free[ei]
                if rc < CHT:
                    evac_wait("vector", bk)
                    if c is not None:
                        plan.wait("vector", c, v)
                    plan.op_inc("vector", SV, 1, lambda e, bk=bk, ei=ei:
                                e.tensor_copy(out=ev[ei], in_=P[bk][:]))
                    evac_done("vector", bk)
                    plan.wait("sync", SV, SV.n)
                    ef = plan.dma("sync", SQ, XZPRE[rc, :, sl], ev[ei])
                else:
                    evac_wait("scalar", bk)
                    if c is not None:
                        plan.wait("scalar", c, v)
                    plan.op_inc("scalar", SS, 1, lambda e, bk=bk, ei=ei:
                                e.activation(out=ev[ei], in_=P[bk][:],
                                             func=AF.Silu))
                    evac_done("scalar", bk)
                    plan.wait("sync", SS, SS.n)
                    ef = plan.dma("sync", SQ, SILUZ[rc - CHT, :, sl], ev[ei])
                ev_free[ei] = (SQ, ef)
            h_free[si] = (ST, ST.n)
        ar.pop()

    def emit_conv_proj():
        ar.push()
        xpt = ar.f32(CHT * 96)
        dpt = ar.f32(DI)
        ccarry = ar.f32(CHT * 3)
        cacc = ar.f32(TC)
        xip = [ar.f32(CHT * TC) for _ in range(2)]
        wrk = ar.f32(515)
        xia = ar.f32(CHT * TC)
        dtb64 = ar.f32(TC)
        ev = [ar.f32(TC) for _ in range(2)]
        xip_free = [(None, 0), (None, 0)]
        ev_free = [(None, 0), (None, 0)]
        xia_stored = 0
        plan.dma("sync", SQ, xpt.rearrange("p (c r) -> p c r", c=CHT),
                 XPT.rearrange("(c p) r -> c p r", p=128).transpose([1, 0, 2]))
        wld = plan.dma("sync", SQ, dpt[0:DTR, :], DPT)
        plan.wait("tensor", SQ, wld)
        for tc in range(NTC):
            si = tc % 2
            c, v = xip_free[si]
            if c is not None:
                plan.wait("sync", c, v)
            sl = slice(tc * TC, (tc + 1) * TC)
            xld = plan.dma("sync", SQ,
                           xip[si].rearrange("p (c t) -> p c t", c=CHT),
                           XZPRE[:, :, sl].transpose([1, 0, 2]))
            plan.wait("vector", SQ, xld)
            plan.wait("vector", SQ, xia_stored)
            for rc in range(CHT):
                xr = xip[si][:, rc * TC:(rc + 1) * TC]
                xo = xia[:, rc * TC:(rc + 1) * TC]
                if tc % 4 == 0:
                    plan.op("vector", lambda e, rc=rc: e.memset(
                        ccarry[:, rc * 3:(rc + 1) * 3], 0.0))
                plan.wait("vector", SS, SS.n)
                plan.op("vector", lambda e, rc=rc: e.tensor_copy(
                    out=wrk[:, 0:3], in_=ccarry[:, rc * 3:(rc + 1) * 3]))
                plan.op("vector", lambda e, xr=xr: e.tensor_copy(
                    out=wrk[:, 3:515], in_=xr))
                plan.op("vector", lambda e, rc=rc, xr=xr: e.tensor_copy(
                    out=ccarry[:, rc * 3:(rc + 1) * 3], in_=xr[:, 509:512]))
                plan.op("vector", lambda e, rc=rc: e.tensor_scalar(
                    out=cacc, in0=wrk[:, 3:515],
                    scalar1=cw[:, rc * 4 + 3:rc * 4 + 4], scalar2=None,
                    op0=ALU.mult))
                for j in range(3):
                    plan.op("vector",
                            lambda e, rc=rc, j=j: e.scalar_tensor_tensor(
                                out=cacc, in0=wrk[:, j:j + 512],
                                scalar=cw[:, rc * 4 + j:rc * 4 + j + 1],
                                in1=cacc, op0=ALU.mult, op1=ALU.add))
                plan.op_inc("vector", SV, 1, lambda e: e.drain())
                plan.wait("scalar", SV, SV.n)
                plan.op_inc("scalar", SS, 1, lambda e, rc=rc, xo=xo:
                            e.activation(out=xo, in_=cacc, func=AF.Silu,
                                         bias=cb[:, rc:rc + 1]))
            silu_done = SS.n
            plan.wait("sync", SS, silu_done)
            xia_stored = plan.dma("sync", SQ,
                                  XIACT[:, :, sl].transpose([1, 0, 2]),
                                  xia.rearrange("p (c t) -> p c t", c=CHT))
            xip_free[si] = (SV, SV.n)
            plan.wait("tensor", SS, silu_done)
            mm_bank(4)
            mm_bank(5)
            for rc in range(CHT):
                plan.op_inc("tensor", ST, 1, lambda e, rc=rc,
                            st=(rc == 0), sp=(rc == CHT - 1): e.matmul(
                                P[4][0:DTR, :],
                                xpt[:, rc * 96:rc * 96 + DTR],
                                xia[:, rc * TC:(rc + 1) * TC],
                                start=st, stop=sp))
            bank_done(4)
            for rc in range(CHT):
                plan.op_inc("tensor", ST, 1, lambda e, rc=rc,
                            st=(rc == 0), sp=(rc == CHT - 1): e.matmul(
                                P[5][0:32, :],
                                xpt[:, rc * 96 + DTR:rc * 96 + 96],
                                xia[:, rc * TC:(rc + 1) * TC],
                                start=st, stop=sp))
            bank_done(5)
            evac_wait("vector", 4)
            plan.op_inc("vector", SV, 1, lambda e: e.tensor_copy(
                out=dtb64[0:DTR, :], in_=P[4][0:DTR, :]))
            evac_done("vector", 4)
            dtev = SV.n
            evac_wait("vector", 5)
            plan.op_inc("vector", SV, 1, lambda e, sl=sl: e.tensor_copy(
                out=bc[:, sl], in_=P[5][0:32, :]))
            evac_done("vector", 5)
            plan.wait("tensor", SV, dtev)
            for oc in range(CHT):
                bk = oc % 4
                mm_bank(bk)
                plan.op_inc("tensor", ST, 1, lambda e, oc=oc, bk=bk: e.matmul(
                    P[bk][:], dpt[0:DTR, oc * 128:(oc + 1) * 128],
                    dtb64[0:DTR, :], start=True, stop=True))
                bank_done(bk)
                evac_wait("vector", bk)
                ei = oc % 2
                c, v = ev_free[ei]
                if c is not None:
                    plan.wait("vector", c, v)
                plan.op_inc("vector", SV, 1, lambda e, bk=bk, ei=ei:
                            e.tensor_copy(out=ev[ei], in_=P[bk][:]))
                evac_done("vector", bk)
                plan.wait("sync", SV, SV.n)
                ef = plan.dma("sync", SQ, DTPRE[oc, :, sl], ev[ei])
                ev_free[ei] = (SQ, ef)
        ar.pop()

    def emit_scan():
        ar.push()
        bt = ar.f32(DS * LC)
        ct = ar.f32(DS * LC)
        xi_t = [ar.f32(LC) for _ in range(2)]
        sz_t = [ar.f32(LC) for _ in range(2)]
        dt_t = [ar.f32(LC) for _ in range(2)]
        sp1 = ar.f32(LC)
        delta = ar.f32(LC)
        du = ar.f32(LC)
        da = ar.f32(LC + 1)
        db = ar.f32(LC + 1)
        sco = ar.f32(LC + 1)
        sco2 = ar.f32(LC + 1)
        tmp = ar.f32(LC)
        acc = ar.f32(LC)
        tmpg = ar.f32(LC)
        accg = ar.f32(LC)
        states = ar.f32(CHT * DS)
        yo = [ar.f32(LC) for _ in range(2)]
        st_free = [(None, 0), (None, 0)]
        yo_free = [(None, 0), (None, 0)]
        prev_merge = [0]
        plan.dma("sync", SQ, sel, SEL)
        plan.wait("tensor", SQ, SQ.n)
        plan.op("vector", lambda e: e.memset(da[:, 0:1], 0.0))
        prev_lc_v = 0
        for b in range(NB):
            plan.op_inc("vector", SV, 1, lambda e: e.memset(states, 0.0))
            for lc in range(NLC):
                t0 = b * L + lc * LC
                sl = slice(t0, t0 + LC)
                plan.wait("scalar", SV, prev_lc_v)
                for n in range(2 * DS):
                    bk = n % 2
                    mm_bank(bk)
                    plan.op_inc("tensor", ST, 1, lambda e, n=n, bk=bk, sl=sl:
                                e.matmul(P[bk][:],
                                         sel[:, n * 128:(n + 1) * 128],
                                         bc[:, sl], start=True, stop=True))
                    bank_done(bk)
                    evac_wait("scalar", bk)
                    dstv = (bt if n < DS else ct)
                    nn = n % DS
                    plan.op_inc("scalar", SS, 1, lambda e, bk=bk, dstv=dstv,
                                nn=nn: e.activation(
                                    out=dstv[:, nn * LC:(nn + 1) * LC],
                                    in_=P[bk][:], func=AF.Identity))
                    evac_done("scalar", bk)
                btct_done = SS.n
                for cht in range(CHT):
                    si = cht % 2
                    c, v = st_free[si]
                    if c is not None:
                        plan.wait("sync", c, v)
                    plan.dma("sync", SQ, xi_t[si], XIACT[cht, :, sl])
                    plan.dma("sync", SQ, sz_t[si], SILUZ[cht, :, sl])
                    ldd = plan.dma("sync", SQ, dt_t[si], DTPRE[cht, :, sl])
                    plan.wait("scalar", SQ, ldd)
                    plan.wait("scalar", SV, SV.n)
                    plan.op("scalar", lambda e, si=si, cht=cht: e.activation(
                        out=sp1, in_=dt_t[si], func=AF.Abs,
                        bias=dtb[:, cht:cht + 1]))
                    plan.op("scalar", lambda e: e.activation(
                        out=sp1, in_=sp1, func=AF.Exp, scale=-1.0))
                    plan.op("scalar", lambda e: e.activation(
                        out=sp1, in_=sp1, func=AF.Ln, bias=1.0))
                    plan.op_inc("scalar", SS, 1,
                                lambda e, si=si, cht=cht: e.activation(
                                    out=delta, in_=dt_t[si], func=AF.Relu,
                                    bias=dtb[:, cht:cht + 1]))
                    sp_done = SS.n
                    plan.wait("vector", SS, sp_done)
                    plan.op("vector", lambda e: e.tensor_tensor(
                        out=delta, in0=delta, in1=sp1, op=ALU.add))
                    plan.op_inc("vector", SV, 1,
                                lambda e, si=si: e.tensor_tensor(
                                    out=du, in0=delta, in1=xi_t[si],
                                    op=ALU.mult))
                    delta_done = SV.n
                    plan.wait("scalar", SV, delta_done)
                    plan.wait("vector", SS, btct_done)
                    sp_hist = []
                    for n in range(DS):
                        colidx = cht * DS + n
                        odd = (n % 2 == 1)
                        so = sco2 if odd else sco
                        plan.wait("scalar", SV, SV.n)
                        plan.op_inc("scalar", SS, 1,
                                    lambda e, colidx=colidx: e.activation(
                                        out=da[:, 1:LC + 1], in_=delta,
                                        func=AF.Exp,
                                        scale=aneg[:, colidx:colidx + 1]))
                        da_done = SS.n
                        plan.op("vector", lambda e, colidx=colidx: e.tensor_copy(
                            out=db[:, 0:1], in_=states[:, colidx:colidx + 1]))
                        plan.op("vector", lambda e, n=n: e.tensor_tensor(
                            out=db[:, 1:LC + 1], in0=du,
                            in1=bt[:, n * LC:(n + 1) * LC], op=ALU.mult))
                        plan.wait("vector", SS, da_done)
                        if odd and len(sp_hist) >= 2:
                            plan.wait("vector", SP, sp_hist[-2])
                        if odd:
                            plan.op_inc("vector", SV, 1,
                                        lambda e, so=so: e.tensor_tensor_scan(
                                            out=so[:], data0=da[:], data1=db[:],
                                            initial=0.0, op0=ALU.mult,
                                            op1=ALU.add))
                            sv_scan = SV.n
                        else:
                            plan.op("vector", lambda e, so=so: e.tensor_tensor_scan(
                                out=so[:], data0=da[:], data1=db[:],
                                initial=0.0, op0=ALU.mult, op1=ALU.add))
                        plan.op("vector",
                                lambda e, colidx=colidx, so=so: e.tensor_copy(
                                    out=states[:, colidx:colidx + 1],
                                    in_=so[:, LC:LC + 1]))
                        if odd:
                            plan.wait("gpsimd", SV, sv_scan)
                            if n == 1:
                                plan.wait("gpsimd", SV, prev_merge[0])
                                plan.op_inc("gpsimd", SP, 1,
                                            lambda e, n=n: e.tensor_tensor(
                                                out=accg,
                                                in0=sco2[:, 1:LC + 1],
                                                in1=ct[:, n * LC:(n + 1) * LC],
                                                op=ALU.mult))
                            else:
                                plan.op("gpsimd", lambda e, n=n: e.tensor_tensor(
                                    out=tmpg, in0=sco2[:, 1:LC + 1],
                                    in1=ct[:, n * LC:(n + 1) * LC],
                                    op=ALU.mult))
                                plan.op_inc("gpsimd", SP, 1,
                                            lambda e: e.tensor_tensor(
                                                out=accg, in0=accg, in1=tmpg,
                                                op=ALU.add))
                            sp_hist.append(SP.n)
                        else:
                            if n == 0:
                                plan.op_inc("vector", SV, 1,
                                            lambda e, n=n: e.tensor_tensor(
                                                out=acc, in0=sco[:, 1:LC + 1],
                                                in1=ct[:, n * LC:(n + 1) * LC],
                                                op=ALU.mult))
                            else:
                                plan.op("vector", lambda e, n=n: e.tensor_tensor(
                                    out=tmp, in0=sco[:, 1:LC + 1],
                                    in1=ct[:, n * LC:(n + 1) * LC], op=ALU.mult))
                                plan.op_inc("vector", SV, 1,
                                            lambda e: e.tensor_tensor(
                                                out=acc, in0=acc, in1=tmp,
                                                op=ALU.add))
                    plan.wait("vector", SP, sp_hist[-1])
                    plan.op_inc("vector", SV, 1, lambda e: e.tensor_tensor(
                        out=acc, in0=acc, in1=accg, op=ALU.add))
                    prev_merge[0] = SV.n
                    oi = cht % 2
                    c, v = yo_free[oi]
                    if c is not None:
                        plan.wait("vector", c, v)
                    plan.op("vector", lambda e, si=si, cht=cht:
                            e.scalar_tensor_tensor(
                                out=tmp, in0=xi_t[si],
                                scalar=ddi[:, cht:cht + 1], in1=acc,
                                op0=ALU.mult, op1=ALU.add))
                    plan.op_inc("vector", SV, 1, lambda e, si=si, oi=oi:
                                e.tensor_tensor(out=yo[oi], in0=tmp,
                                                in1=sz_t[si], op=ALU.mult))
                    st_free[si] = (SV, SV.n)
                    plan.wait("sync", SV, SV.n)
                    yf = plan.dma("sync", SQ, YCH[cht, :, sl], yo[oi])
                    yo_free[oi] = (SQ, yf)
                prev_lc_v = SV.n
        ar.pop()

    def emit_outproj():
        ar.push()
        woh = ar.bf16(CHT * D)
        wol = ar.bf16(CHT * D)
        wst = [ar.f32(D) for _ in range(2)]
        yh = ar.bf16(CHT * TC)
        yl = ar.bf16(CHT * TC)
        yf = [ar.f32(TC) for _ in range(2)]
        xres = ar.f32(4 * D)
        lres = ar.f32(4 * D)
        xo = [ar.f32(512) for _ in range(2)]
        stage_free = [(None, 0), (None, 0)]
        yf_free = [(None, 0), (None, 0)]
        xo_free = [(None, 0), (None, 0)]
        res_free = (None, 0)
        ysplit_t = 0
        for kc in range(CHT):
            si = kc % 2
            c, v = stage_free[si]
            if c is not None:
                plan.wait("sync", c, v)
            ld = plan.dma("sync", SQ, wst[si], WOT[kc * 128:(kc + 1) * 128, :])
            plan.wait("scalar", SQ, ld)
            plan.op_inc("scalar", SS, 1, lambda e, si=si, kc=kc: e.activation(
                out=woh[:, kc * D:(kc + 1) * D], in_=wst[si], func=AF.Identity))
            plan.wait("vector", SS, SS.n)
            sf = plan.op_inc("vector", SV, 1, lambda e, si=si, kc=kc:
                             e.tensor_tensor(out=wol[:, kc * D:(kc + 1) * D],
                                             in0=wst[si],
                                             in1=woh[:, kc * D:(kc + 1) * D],
                                             op=ALU.subtract))
            stage_free[si] = (SV, sf)
        plan.wait("tensor", SV, SV.n)
        plan.wait("tensor", SS, SS.n)
        for tc in range(NTC):
            sl = slice(tc * TC, (tc + 1) * TC)
            c, v = res_free
            if c is not None:
                plan.wait("sync", c, v)
            for cht in range(CHT):
                si = cht % 2
                c, v = yf_free[si]
                if c is not None:
                    plan.wait("sync", c, v)
                ld = plan.dma("sync", SQ, yf[si], YCH[cht, :, sl])
                plan.wait("scalar", SQ, ld)
                plan.wait("scalar", ST, ysplit_t)
                plan.wait("vector", ST, ysplit_t)
                plan.op_inc("scalar", SS, 1, lambda e, si=si, cht=cht:
                            e.activation(out=yh[:, cht * TC:(cht + 1) * TC],
                                         in_=yf[si], func=AF.Identity))
                plan.wait("vector", SS, SS.n)
                ff = plan.op_inc("vector", SV, 1, lambda e, si=si, cht=cht:
                                 e.tensor_tensor(
                                     out=yl[:, cht * TC:(cht + 1) * TC],
                                     in0=yf[si],
                                     in1=yh[:, cht * TC:(cht + 1) * TC],
                                     op=ALU.subtract))
                yf_free[si] = (SV, ff)
            ysplit = SV.n
            plan.dma("sync", SQ, xres.rearrange("p (t d) -> p t d", t=4),
                     X1[sl, :].rearrange("(t p) d -> t p d", p=128)
                     .transpose([1, 0, 2]))
            rl = plan.dma("sync", SQ, lres.rearrange("p (t d) -> p t d", t=4),
                          LNTM[sl, :].rearrange("(t p) d -> t p d", p=128)
                          .transpose([1, 0, 2]))
            plan.wait("tensor", SV, ysplit)
            plan.wait("vector", SQ, rl)
            for tt in range(4):
                for dh in range(2):
                    bk = 2 + (tt * 2 + dh) % 4
                    mm_bank(bk)
                    for kc in range(CHT):
                        for p_i, (xa, wa) in enumerate(
                                ((yh, woh), (yh, wol), (yl, woh))):
                            plan.op_inc(
                                "tensor", ST, 1,
                                lambda e, xa=xa, wa=wa, kc=kc, tt=tt, dh=dh,
                                bk=bk, st=(kc == 0 and p_i == 0),
                                sp=(kc == CHT - 1 and p_i == 2): e.matmul(
                                    P[bk][:],
                                    xa[:, kc * TC + tt * 128:
                                       kc * TC + (tt + 1) * 128],
                                    wa[:, kc * D + dh * 512:
                                       kc * D + (dh + 1) * 512],
                                    start=st, stop=sp))
                    bank_done(bk)
                    evac_wait("vector", bk)
                    oi = (tt * 2 + dh) % 2
                    c, v = xo_free[oi]
                    if c is not None:
                        plan.wait("vector", c, v)
                    plan.op("vector", lambda e, bk=bk, oi=oi, tt=tt, dh=dh:
                            e.tensor_tensor(
                                out=xo[oi], in0=P[bk][:],
                                in1=lres[:, tt * D + dh * 512:
                                         tt * D + (dh + 1) * 512],
                                op=ALU.add))
                    plan.op_inc("vector", SV, 1,
                                lambda e, oi=oi, tt=tt, dh=dh: e.tensor_tensor(
                                    out=xo[oi], in0=xo[oi],
                                    in1=xres[:, tt * D + dh * 512:
                                             tt * D + (dh + 1) * 512],
                                    op=ALU.add))
                    evac_done("vector", bk)
                    plan.wait("sync", SV, SV.n)
                    of = plan.dma("sync", SQ,
                                  X2[tc * TC + tt * 128:
                                     tc * TC + (tt + 1) * 128,
                                     dh * 512:(dh + 1) * 512], xo[oi])
                    xo_free[oi] = (SQ, of)
            ysplit_t = ST.n
            res_free = (SV, SV.n)
        ar.pop()

    def warmup(func):
        plan.op("scalar", lambda e: e.memzero(wcol[:, 0:1]))
        plan.op("scalar", lambda e: e.drain())
        plan.op("scalar", lambda e, func=func: e.activation(
            out=wcol[:, 1:2], in_=wcol[:, 0:1], func=func))
        plan.op("scalar", lambda e: e.nop(cycle_cnt=8000))

    def dbg_dump(vall):
        if not _DEBUG:
            return
        plan.wait("gpsimd", SV, SV.n)
        plan.wait("gpsimd", SS, SS.n)
        plan.wait("gpsimd", ST, ST.n)
        plan.dma("gpsimd", SG, DBG_SLAB, slab[:, :])
        plan.dma("gpsimd", SG, DBG_V, vall)
        plan.dma("gpsimd", SG, DBG_TM[:, 0:256], ptm[:])
        plan.dma("gpsimd", SG, DBG_TM[:, 256:512], indtm[:])
        plan.dma("gpsimd", SG, DBG_TM[:, 512:768], wtk[:])
        plan.wait("gpsimd", SG, SG.n)

    emit_const_loads()
    ar.push()
    vall = ar.f32(D)[0:E, :]
    vstage = ar.f32(D)[0:1, :]
    b3row = ar.f32(D)[0:1, :]
    emit_norm_phase(X, rms=False, gate=True, store_lntm=False, nw_scale=False)
    emit_gate_post()
    if upto >= 1:
        emit_ffn_experts(vstage, b3row)
        emit_allreduce(vall)
        emit_ffn_apply(X, X1, vall)
    dbg_dump(vall)
    ar.pop()
    barrier()
    with nc.named_scope("rb1_ffn1"):
        plan.flush(nc)
    if upto <= 1:
        return nc

    warmup(AF.Exp)
    emit_norm_phase(X1, rms=True, gate=False, store_lntm=True, nw_scale=True)
    barrier()
    with nc.named_scope("rb2_norm"):
        plan.flush(nc)
    if upto <= 2:
        return nc

    warmup(AF.Silu)
    emit_inproj()
    barrier()
    with nc.named_scope("rb3_inproj"):
        plan.flush(nc)
    if upto <= 3:
        return nc

    warmup(AF.Silu)
    emit_conv_proj()
    barrier()
    with nc.named_scope("rb4_convproj"):
        plan.flush(nc)
    if upto <= 4:
        return nc

    warmup(AF.Exp)
    emit_scan()
    barrier()
    with nc.named_scope("rb5_scan"):
        plan.flush(nc)
    if upto <= 5:
        return nc

    warmup(AF.Identity)
    emit_outproj()
    barrier()
    with nc.named_scope("rb6_outproj"):
        plan.flush(nc)
    if upto <= 6:
        return nc

    ar.push()
    vall = ar.f32(D)[0:E, :]
    vstage = ar.f32(D)[0:1, :]
    b3row = ar.f32(D)[0:1, :]
    warmup(AF.Exp)
    emit_norm_phase(X2, rms=False, gate=True, store_lntm=False, nw_scale=False)
    emit_gate_post()
    emit_ffn_experts(vstage, b3row)
    emit_allreduce(vall)
    emit_ffn_apply(X2, XOUT, vall)
    ar.pop()
    plan.wait("sync", SQ, SQ.n)
    barrier()
    with nc.named_scope("rb7_ffn2"):
        plan.flush(nc)

    return nc


def prep_inputs(inputs):
    x = np.ascontiguousarray(np.asarray(inputs["L0_attn_x"], np.float32))
    gw = np.ascontiguousarray(np.asarray(inputs["L0_ffn_gate_w"], np.float32))
    gb = np.asarray(inputs["L0_ffn_gate_b"], np.float32).reshape(E, 1)
    w1 = np.ascontiguousarray(np.asarray(inputs["L0_ffn_w1"], np.float32))
    w2 = np.ascontiguousarray(np.asarray(inputs["L0_ffn_w2"], np.float32))
    w3 = np.ascontiguousarray(np.asarray(inputs["L0_ffn_w3"], np.float32))
    b1 = np.ascontiguousarray(
        np.asarray(inputs["L0_ffn_b1"], np.float32).reshape(E, 8, 128)
        .transpose(0, 2, 1))
    b2 = np.ascontiguousarray(
        np.asarray(inputs["L0_ffn_b2"], np.float32).reshape(E, 8, 128)
        .transpose(0, 2, 1))
    b3 = np.ascontiguousarray(np.asarray(inputs["L0_ffn_b3"], np.float32))
    nw = np.ascontiguousarray(
        np.asarray(inputs["L0_attn_norm_w"], np.float32).reshape(DCH, 128).T)
    wit = np.ascontiguousarray(
        np.asarray(inputs["L0_attn_in_proj_w"], np.float32).T)
    cwh = np.ascontiguousarray(
        np.asarray(inputs["L0_attn_conv_w"], np.float32)
        .reshape(CHT, 128, 4).transpose(1, 0, 2))
    cbh = np.ascontiguousarray(
        np.asarray(inputs["L0_attn_conv_b"], np.float32).reshape(CHT, 128).T)
    xpt = np.ascontiguousarray(
        np.asarray(inputs["L0_attn_x_proj_w"], np.float32).T)
    dpt = np.ascontiguousarray(
        np.asarray(inputs["L0_attn_dt_proj_w"], np.float32).T)
    dtbh = np.ascontiguousarray(
        np.asarray(inputs["L0_attn_dt_proj_b"], np.float32).reshape(CHT, 128).T)
    alog = np.ascontiguousarray(
        np.asarray(inputs["L0_attn_A_log"], np.float32)
        .reshape(CHT, 128, DS).transpose(1, 0, 2).reshape(128, CHT * DS))
    ddi = np.ascontiguousarray(
        np.asarray(inputs["L0_attn_D"], np.float32).reshape(CHT, 128).T)
    wot = np.ascontiguousarray(
        np.asarray(inputs["L0_attn_out_proj_w"], np.float32).T)
    idt = np.eye(128, dtype=np.float32)
    one1 = np.ones((1, 128), dtype=np.float32)
    onec = np.ones((128, 1), dtype=np.float32)
    selm = np.ascontiguousarray(
        np.kron(np.eye(32, dtype=np.float32),
                np.ones((1, 128), dtype=np.float32)))
    shared = dict(GW=gw, GB=gb, W1=w1, W2=w2, W3=w3, B1=b1, B2=b2, B3=b3,
                  NW=nw, WIT=wit, CW=cwh, CB=cbh, XPT=xpt, DPT=dpt, DTB=dtbh,
                  ALOG=alog, DDI=ddi, WOT=wot, IDT=idt, ONE1=one1, ONEC=onec,
                  SEL=selm)
    in_maps = []
    for c in range(NCORES):
        m = dict(shared)
        m["X"] = np.ascontiguousarray(x[2 * c: 2 * c + 2].reshape(T, D))
        in_maps.append(m)
    return in_maps


def kernel(**inputs):
    in_maps = prep_inputs(inputs)
    nc = build_program()
    res = run_bass_kernel_spmd(nc, in_maps, core_ids=list(range(NCORES)))
    out = np.concatenate(
        [np.asarray(r["XOUT"]).reshape(NB, L, D) for r in res.results], axis=0)
    return out


# version 03
# speedup vs baseline: 1.0356x; 1.0356x over previous
import numpy as np
import concourse.bass as bass
import concourse.mybir as mybir
from concourse.bass_utils import run_bass_kernel_spmd

F32 = mybir.dt.float32
BF16 = mybir.dt.bfloat16
AF = mybir.ActivationFunctionType
ALU = mybir.AluOpType
AX = mybir.AxisListType

NCORES = 8
T = 4096
L = 2048
NB = 2
D = 1024
E = 8
DI = 2048
DS = 16
DTR = 64
TC = 512
NTC = T // TC
DCH = D // 128
CHT = DI // 128
LC = 512
NLC = L // LC
EPS = 1e-5

_DEBUG = False


class Ctr:
    def __init__(self, nc, name):
        self.h = nc.alloc_semaphore(name)
        self.n = 0


class Plan:
    ENGS = ("sync", "tensor", "vector", "scalar", "gpsimd")

    def __init__(self):
        self.ops = {e: [] for e in self.ENGS}
        self.last_wait = {}

    def op(self, eng, fn):
        self.ops[eng].append(fn)

    def op_inc(self, eng, ctr, k, fn):
        def g(e, fn=fn, h=ctr.h, k=k):
            fn(e).then_inc(h, k)
        self.ops[eng].append(g)
        ctr.n += k
        return ctr.n

    def wait(self, eng, ctr, val):
        if val is None or val <= 0:
            return
        key = (eng, id(ctr))
        if self.last_wait.get(key, -1) >= val:
            return
        self.last_wait[key] = val
        self.ops[eng].append(lambda e, h=ctr.h, v=val: e.wait_ge(h, v))

    def dma(self, eng, ctr, out, in_):
        return self.op_inc(eng, ctr, 16,
                           lambda e, o=out, i=in_: e.dma_start(out=o, in_=i))

    def flush(self, nc):
        with nc.Block() as blk:
            for eng in self.ENGS:
                ops = self.ops[eng]
                if ops:
                    getattr(blk, eng)(
                        lambda e, ops=tuple(ops): [f(e) for f in ops])
        self.ops = {e: [] for e in self.ENGS}
        self.last_wait = {}


def build_program(upto=99):
    nc = bass.Bass("TRN2", target_bir_lowering=False, debug=False)

    def din(name, shape):
        return nc.dram_tensor(name, list(shape), F32, kind="ExternalInput").ap()

    X = din("X", (T, D))
    GW = din("GW", (D, E))
    GB = din("GB", (E, 1))
    W1 = din("W1", (E, D, D))
    W2 = din("W2", (E, D, D))
    W3 = din("W3", (E, D, D))
    B1 = din("B1", (E, 128, 8))
    B2 = din("B2", (E, 128, 8))
    B3 = din("B3", (E, D))
    NW = din("NW", (128, DCH))
    WIT = din("WIT", (D, 2 * DI))
    CW = din("CW", (128, CHT, 4))
    CB = din("CB", (128, CHT))
    XPT = din("XPT", (DI, 96))
    DPT = din("DPT", (DTR, DI))
    DTB = din("DTB", (128, CHT))
    ALOG = din("ALOG", (128, CHT * DS))
    DDI = din("DDI", (128, CHT))
    WOT = din("WOT", (DI, D))
    IDT = din("IDT", (128, 128))
    ONE1 = din("ONE1", (1, 128))
    ONEC = din("ONEC", (128, 1))
    SEL = din("SEL", (32, 32 * 128))

    XOUT = nc.dram_tensor("XOUT", [T, D], F32, kind="ExternalOutput").ap()
    skind = "ExternalOutput" if _DEBUG else "Internal"

    def dsc(name, shape, dt=F32):
        return nc.dram_tensor(name, list(shape), dt, kind=skind).ap()

    X1 = dsc("X1", (T, D))
    X2 = dsc("X2", (T, D))
    LNTM = dsc("LNTM", (T, D))
    AFH = dsc("AFH", (DCH, 128, T), BF16)
    AFL = dsc("AFL", (DCH, 128, T), BF16)
    XZPRE = dsc("XZPRE", (CHT, 128, T))
    SILUZ = dsc("SILUZ", (CHT, 128, T))
    XIACT = dsc("XIACT", (CHT, 128, T))
    DTPRE = dsc("DTPRE", (CHT, 128, T))
    YCH = dsc("YCH", (CHT, 128, T))
    CCIN = nc.dram_tensor("CCIN", [E, D], F32)
    CCOUT = nc.dram_tensor("CCOUT", [E, D], F32)
    if _DEBUG:
        DBG_SLAB = nc.dram_tensor("DBG_SLAB", [128, 2 * T], F32,
                                  kind="ExternalOutput").ap()
        DBG_V = nc.dram_tensor("DBG_V", [E, D], F32, kind="ExternalOutput").ap()
        DBG_TM = nc.dram_tensor("DBG_TM", [128, 3 * 256], F32,
                                kind="ExternalOutput").ap()

    def sb(name, shape, dt=F32):
        return nc.alloc_sbuf_tensor(name, list(shape), dt)

    idt = sb("idt", (128, 128))
    one1 = sb("one1", (1, 128))
    onec = sb("onec", (128, 1))
    gw = sb("gw", (128, DCH * E))
    gb = sb("gb", (E, 1))
    b1c = sb("b1c", (128, E * 8))
    b2c = sb("b2c", (128, E * 8))
    nw = sb("nw", (128, DCH))
    cw = sb("cw", (128, CHT * 4))
    cb = sb("cb", (128, CHT))
    dtb = sb("dtb", (128, CHT))
    aneg = sb("aneg", (128, CHT * DS))
    ddi = sb("ddi", (128, CHT))
    ptm = sb("ptm", (128, 32 * E))
    indtm = sb("indtm", (128, 32 * E))
    wtk = sb("wtk", (128, 32 * E))
    colbuf = sb("colbuf", (128, 16))
    msk_sb = sb("msk_sb", (128, 8))
    col8 = sb("col8", (E, 4))
    epscol = sb("epscol", (128, 1))
    wcol = sb("wcol", (128, 2))
    slab = sb("slab", (128, 2 * T))
    logits = slab[0:E, 0:T]
    probs = logits
    ind_em = slab[0:E, T:2 * T]
    bc = slab[0:32, 0:T]
    sel = slab[0:32, T:2 * T]

    AR_W = 43300
    arena = sb("arena", (128, AR_W))

    class Arena:
        def __init__(self):
            self.off = 0
            self.marks = []

        def f32(self, ncols):
            a = self.off
            self.off += ncols
            assert self.off <= AR_W, f"arena overflow {self.off}"
            return arena[:, a:a + ncols]

        def bf16(self, ncols):
            assert ncols % 2 == 0
            return self.f32(ncols // 2).bitcast(BF16)

        def push(self):
            self.marks.append(self.off)

        def pop(self):
            self.off = self.marks.pop()

    ar = Arena()

    P = [nc.alloc_psum_tensor(f"pp{i}", [128, 512], F32) for i in range(6)]
    PV = nc.alloc_psum_tensor("pv", [1, 1024], F32)

    SQ = Ctr(nc, "SQ")
    SG = Ctr(nc, "SG")
    ST = Ctr(nc, "ST")
    SV = Ctr(nc, "SV")
    SS = Ctr(nc, "SS")
    SC = Ctr(nc, "SC")
    SP = Ctr(nc, "SP")
    SPH = Ctr(nc, "SPH")

    plan = Plan()

    psum_state = {i: (None, 0) for i in range(6)}
    psum_ready = {i: 0 for i in range(6)}
    pv_state = [(None, 0)]

    def mm_bank(i):
        c, v = psum_state[i]
        if c is not None:
            plan.wait("tensor", c, v)

    def bank_done(i):
        psum_ready[i] = ST.n

    def evac_wait(eng, i):
        plan.wait(eng, ST, psum_ready[i])

    def evac_done(eng, i):
        ctr = {"vector": SV, "scalar": SS}[eng]
        psum_state[i] = (ctr, ctr.n)

    def barrier():
        for eng in Plan.ENGS:
            plan.op_inc(eng, SPH, 1, lambda e: e.nop())
        tgt = SPH.n
        for eng in Plan.ENGS:
            plan.wait(eng, SPH, tgt)

    def emit_const_loads():
        plan.dma("sync", SQ, idt[:], IDT)
        plan.dma("sync", SQ, one1[:], ONE1)
        plan.dma("sync", SQ, onec[:], ONEC)
        plan.dma("sync", SQ, gw[:].rearrange("p (c e) -> p c e", c=DCH),
                 GW.rearrange("(c p) e -> c p e", p=128).transpose([1, 0, 2]))
        plan.dma("sync", SQ, gb[:], GB)
        plan.dma("sync", SQ, b1c[:].rearrange("p (e o) -> p e o", e=E),
                 B1.transpose([1, 0, 2]))
        plan.dma("sync", SQ, b2c[:].rearrange("p (e o) -> p e o", e=E),
                 B2.transpose([1, 0, 2]))
        plan.dma("sync", SQ, nw[:], NW)
        plan.dma("sync", SQ, cw[:].rearrange("p (c j) -> p c j", c=CHT), CW)
        plan.dma("sync", SQ, cb[:], CB)
        plan.dma("sync", SQ, dtb[:], DTB)
        plan.dma("sync", SQ, aneg[:], ALOG)
        plan.dma("sync", SQ, ddi[:], DDI)
        ld = SQ.n
        plan.op("vector", lambda e: e.memset(epscol[:], EPS))
        plan.op("scalar", lambda e: e.memzero(wcol[:, 0:1]))
        plan.op("scalar", lambda e: e.drain())
        plan.op("scalar", lambda e: e.activation(
            out=wcol[:, 1:2], in_=wcol[:, 0:1], func=AF.Exp))
        plan.op("scalar", lambda e: e.nop(cycle_cnt=8000))
        plan.wait("scalar", SQ, ld)
        plan.op_inc("scalar", SS, 1,
                    lambda e: e.activation(out=aneg[:], in_=aneg[:], func=AF.Exp))
        plan.wait("vector", SS, SS.n)
        plan.op("vector", lambda e: e.tensor_scalar(out=aneg[:], in0=aneg[:],
                                                    scalar1=-1.0, scalar2=None,
                                                    op0=ALU.mult))
        plan.op_inc("vector", SV, 1, lambda e: e.drain())
        plan.wait("scalar", SV, SV.n)

    def emit_ln_chunk(xin, lnbuf, rms, lnsq):
        m = colbuf[:, 0:1]
        s2 = colbuf[:, 1:2]
        s = colbuf[:, 2:3]
        rr = colbuf[:, 3:4]
        nms = colbuf[:, 6:7]
        for tt in range(4):
            xt = xin[:, tt * D:(tt + 1) * D]
            lt = lnbuf[:, tt * D:(tt + 1) * D]
            plan.wait("vector", SS, SS.n)
            plan.op("vector", lambda e, xt=xt: e.tensor_tensor(
                out=lnsq, in0=xt, in1=xt, op=ALU.mult))
            plan.op("vector", lambda e: e.memset(m, 0.0))
            plan.op("vector", lambda e: e.drain())
            plan.op("vector", lambda e, xt=xt: e.tensor_reduce(
                out=m, in_=xt, axis=AX.X, op=ALU.add))
            plan.op("vector", lambda e: e.memset(s2, 0.0))
            plan.op("vector", lambda e: e.drain())
            plan.op("vector", lambda e: e.tensor_reduce(
                out=s2, in_=lnsq, axis=AX.X, op=ALU.add))
            plan.op("vector", lambda e: e.drain())
            plan.op("vector", lambda e: e.tensor_scalar(
                out=m, in0=m, scalar1=1.0 / D, scalar2=None, op0=ALU.mult))
            plan.op("vector", lambda e: e.tensor_scalar(
                out=s2, in0=s2, scalar1=1.0 / D, scalar2=None, op0=ALU.mult))
            plan.op("vector", lambda e: e.drain())
            plan.op("vector", lambda e: e.tensor_scalar(
                out=rr, in0=m, scalar1=m, scalar2=None, op0=ALU.mult))
            plan.op("vector", lambda e: e.drain())
            plan.op("vector", lambda e: e.tensor_scalar(
                out=s2, in0=s2, scalar1=rr, scalar2=None, op0=ALU.subtract))
            plan.op_inc("vector", SV, 1, lambda e: e.drain())
            var_done = SV.n
            plan.wait("scalar", SV, var_done)
            plan.op("scalar", lambda e: e.drain())
            plan.op("scalar", lambda e: e.activation(
                out=s, in_=s2, func=AF.Ln, bias=epscol[:, 0:1]))
            plan.op("scalar", lambda e: e.activation(
                out=s, in_=s, func=AF.Exp, scale=-0.5))
            plan.op("scalar", lambda e: e.drain())
            plan.op("scalar", lambda e: e.activation(
                out=nms, in_=s, func=AF.Identity, scale=m))
            plan.op("scalar", lambda e: e.activation(
                out=nms, in_=nms, func=AF.Identity, scale=-1.0))
            plan.op("scalar", lambda e: e.drain())
            plan.op_inc("scalar", SS, 1, lambda e, xt=xt, lt=lt: e.activation(
                out=lt, in_=xt, func=AF.Identity, scale=s, bias=nms))
            if rms:
                plan.wait("vector", SS, SS.n)
                plan.op("vector", lambda e, lt=lt, lnsq=lnsq: e.tensor_tensor(
                    out=lnsq, in0=lt, in1=lt, op=ALU.mult))
                plan.op("vector", lambda e: e.memset(rr, 0.0))
                plan.op("vector", lambda e: e.drain())
                plan.op("vector", lambda e: e.tensor_reduce(
                    out=rr, in_=lnsq, axis=AX.X, op=ALU.add))
                plan.op("vector", lambda e: e.drain())
                plan.op("vector", lambda e: e.tensor_scalar(
                    out=rr, in0=rr, scalar1=1.0 / D, scalar2=None, op0=ALU.mult))
                plan.op_inc("vector", SV, 1, lambda e: e.drain())
                plan.wait("scalar", SV, SV.n)
                plan.op("scalar", lambda e: e.drain())
                plan.op("scalar", lambda e: e.activation(
                    out=rr, in_=rr, func=AF.Ln, bias=epscol[:, 0:1]))
                plan.op("scalar", lambda e: e.activation(
                    out=rr, in_=rr, func=AF.Exp, scale=-0.5))
                plan.op("scalar", lambda e: e.drain())
                plan.op_inc("scalar", SS, 1, lambda e, lt=lt: e.activation(
                    out=lt, in_=lt, func=AF.Identity, scale=rr))

    def emit_norm_phase(src, rms, gate, store_lntm, nw_scale):
        ar.push()
        xin = [ar.f32(4 * D) for _ in range(2)]
        lnbuf = ar.f32(4 * D)
        lnsq = ar.f32(D)
        fmf = ar.f32(DCH * TC)
        fmh = ar.bf16(DCH * TC)
        fml = ar.bf16(DCH * TC)
        xin_free = [(None, 0), (None, 0)]
        lnbuf_free_t = 0
        lnbuf_free_q = 0
        fm_store_done = 0
        gate_done_t = 0
        for tc in range(NTC):
            sl = slice(tc * TC, (tc + 1) * TC)
            buf = xin[tc % 2]
            c, v = xin_free[tc % 2]
            if c is not None:
                plan.wait("sync", c, v)
            ld = plan.dma("sync", SQ, buf.rearrange("p (t d) -> p t d", t=4),
                          src[sl, :].rearrange("(t p) d -> t p d", p=128)
                          .transpose([1, 0, 2]))
            plan.wait("vector", SQ, ld)
            plan.wait("scalar", SQ, ld)
            plan.wait("vector", ST, lnbuf_free_t)
            plan.wait("vector", SQ, lnbuf_free_q)
            emit_ln_chunk(buf, lnbuf, rms, lnsq)
            ln_done = SS.n
            xin_free[tc % 2] = (SS, ln_done)
            if store_lntm:
                plan.wait("sync", SS, ln_done)
                lnbuf_free_q = plan.dma(
                    "sync", SQ,
                    LNTM[sl, :].rearrange("(t p) d -> t p d", p=128)
                    .transpose([1, 0, 2]),
                    lnbuf.rearrange("p (t d) -> p t d", t=4))
            plan.wait("tensor", SS, ln_done)
            for tt in range(4):
                for half in range(2):
                    b = 4 + half
                    mm_bank(b)
                    for j in range(4):
                        dc = half * 4 + j
                        plan.op_inc(
                            "tensor", ST, 1,
                            lambda e, tt=tt, dc=dc, b=b, j=j: e.transpose(
                                P[b][:, j * 128:(j + 1) * 128],
                                lnbuf[:, tt * D + dc * 128:
                                      tt * D + (dc + 1) * 128],
                                idt[:]))
                    bank_done(b)
                    evac_wait("scalar", b)
                    plan.wait("scalar", SQ, fm_store_done)
                    plan.wait("scalar", ST, gate_done_t)

                    def ev(e, tt=tt, half=half, b=b):
                        return e.activation(
                            out=fmf.rearrange("p (c t) -> p c t", c=DCH)
                            [:, half * 4:(half + 1) * 4,
                             tt * 128:(tt + 1) * 128],
                            in_=P[b][:].rearrange("p (c t) -> p c t", c=4),
                            func=AF.Identity)
                    plan.op_inc("scalar", SS, 1, ev)
                    evac_done("scalar", b)
            lnbuf_free_t = ST.n
            fm_done_c, fm_done_v = SS, SS.n
            if nw_scale:
                plan.wait("vector", SS, SS.n)
                for dc in range(DCH):
                    plan.op_inc("vector", SV, 1, lambda e, dc=dc: e.tensor_scalar(
                        out=fmf[:, dc * TC:(dc + 1) * TC],
                        in0=fmf[:, dc * TC:(dc + 1) * TC],
                        scalar1=nw[:, dc:dc + 1], scalar2=None, op0=ALU.mult))
                fm_done_c, fm_done_v = SV, SV.n
                plan.wait("scalar", SV, SV.n)
            plan.op_inc("scalar", SS, 1,
                        lambda e: e.activation(out=fmh, in_=fmf,
                                               func=AF.Identity))
            hi_done = SS.n
            plan.wait("vector", SS, hi_done)
            plan.wait("vector", SQ, fm_store_done)
            plan.op_inc("vector", SV, 1, lambda e: e.tensor_tensor(
                out=fml, in0=fmf, in1=fmh, op=ALU.subtract))
            lo_done = SV.n
            plan.wait("sync", SS, hi_done)
            plan.wait("sync", SV, lo_done)
            plan.dma("sync", SQ, AFH[:, :, sl].transpose([1, 0, 2]),
                     fmh.rearrange("p (c t) -> p c t", c=DCH))
            fm_store_done = plan.dma(
                "sync", SQ, AFL[:, :, sl].transpose([1, 0, 2]),
                fml.rearrange("p (c t) -> p c t", c=DCH))
            if gate:
                mm_bank(0)
                plan.wait("tensor", fm_done_c, fm_done_v)
                for dc in range(DCH):
                    plan.op_inc("tensor", ST, 1, lambda e, dc=dc, sl=sl: e.matmul(
                        P[0][0:E, :], gw[:, dc * E:(dc + 1) * E],
                        fmf[:, dc * TC:(dc + 1) * TC],
                        start=(dc == 0), stop=(dc == DCH - 1)))
                gate_done_t = ST.n
                bank_done(0)
                evac_wait("vector", 0)
                plan.op_inc("vector", SV, 1, lambda e, sl=sl: e.tensor_scalar(
                    out=logits[:, sl], in0=P[0][0:E, :], scalar1=gb[:, 0:1],
                    scalar2=None, op0=ALU.add))
                evac_done("vector", 0)
        ar.pop()

    def emit_gate_post():
        mx = col8[:, 0:1]
        sm = col8[:, 1:2]
        for b in range(NB):
            sl = slice(b * L, (b + 1) * L)
            plan.op("vector", lambda e: e.memset(mx, -3e38))
            plan.op("vector", lambda e: e.drain())
            plan.op("vector", lambda e, sl=sl: e.tensor_reduce(
                out=mx, in_=logits[:, sl], axis=AX.X, op=ALU.max))
            plan.op("vector", lambda e: e.drain())
            plan.op("vector", lambda e: e.tensor_scalar(
                out=mx, in0=mx, scalar1=-1.0, scalar2=None, op0=ALU.mult))
            plan.op_inc("vector", SV, 1, lambda e: e.drain())
            plan.wait("scalar", SV, SV.n)
            plan.op("scalar", lambda e: e.drain())
            plan.op("scalar", lambda e, sl=sl: e.activation(
                out=probs[:, sl], in_=logits[:, sl], func=AF.Exp, bias=mx,
                accum_out=sm))
            plan.op("scalar", lambda e: e.activation(
                out=col8[:, 2:3], in_=sm, func=AF.Identity))
            plan.op_inc("scalar", SS, 1, lambda e: e.drain())
            plan.wait("vector", SS, SS.n)
            plan.op("vector", lambda e: e.drain())
            plan.op("vector", lambda e: e.reciprocal(out=col8[:, 3:4], in_=sm))
            plan.op("vector", lambda e: e.drain())
            plan.op_inc("vector", SV, 1, lambda e, sl=sl: e.tensor_scalar(
                out=probs[:, sl], in0=probs[:, sl], scalar1=col8[:, 3:4],
                scalar2=None, op0=ALU.mult))
        probs_done = SV.n
        plan.wait("tensor", SV, probs_done)
        mm_bank(4)
        for i in range(32):
            plan.op_inc("tensor", ST, 1, lambda e, i=i: e.transpose(
                P[4][:, i * 8:(i + 1) * 8],
                probs[:, i * 128:(i + 1) * 128], idt[0:E, 0:E]))
        bank_done(4)
        evac_wait("vector", 4)
        plan.op_inc("vector", SV, 1,
                    lambda e: e.tensor_copy(out=ptm[:], in_=P[4][:, 0:256]))
        evac_done("vector", 4)
        m1 = colbuf[:, 4:5]
        m2 = colbuf[:, 5:6]
        msk = msk_sb[:]
        for i in range(32):
            pp = ptm[:, i * 8:(i + 1) * 8]
            it = indtm[:, i * 8:(i + 1) * 8]
            wt = wtk[:, i * 8:(i + 1) * 8]
            plan.op("vector", lambda e: e.memset(m1, -3e38))
            plan.op("vector", lambda e: e.drain())
            plan.op("vector", lambda e, pp=pp: e.tensor_reduce(
                out=m1, in_=pp, axis=AX.X, op=ALU.max))
            plan.op("vector", lambda e: e.drain())
            plan.op("vector", lambda e, pp=pp: e.tensor_scalar(
                out=msk, in0=pp, scalar1=m1, scalar2=-1e30,
                op0=ALU.is_ge, op1=ALU.mult))
            plan.op("vector", lambda e: e.drain())
            plan.op("vector", lambda e, pp=pp: e.tensor_tensor(
                out=msk, in0=msk, in1=pp, op=ALU.add))
            plan.op("vector", lambda e: e.memset(m2, -3e38))
            plan.op("vector", lambda e: e.drain())
            plan.op("vector", lambda e: e.tensor_reduce(
                out=m2, in_=msk, axis=AX.X, op=ALU.max))
            plan.op("vector", lambda e: e.drain())
            plan.op("vector", lambda e, pp=pp, it=it: e.tensor_scalar(
                out=it, in0=pp, scalar1=m2, scalar2=None, op0=ALU.is_ge))
            plan.op("vector", lambda e: e.drain())
            plan.op_inc("vector", SV, 1, lambda e, pp=pp, it=it, wt=wt:
                        e.tensor_tensor(out=wt, in0=pp, in1=it, op=ALU.mult))
        top2_done = SV.n
        plan.wait("tensor", SV, top2_done)
        for g in range(8):
            b = 4 + (g % 2)
            mm_bank(b)
            for j in range(4):
                i = g * 4 + j
                plan.op_inc("tensor", ST, 1, lambda e, i=i, j=j, b=b: e.transpose(
                    P[b][0:E, j * 128:(j + 1) * 128],
                    indtm[:, i * 8:(i + 1) * 8], idt[:]))
            bank_done(b)
            evac_wait("vector", b)
            plan.op_inc("vector", SV, 1, lambda e, g=g, b=b: e.tensor_copy(
                out=ind_em[:, g * 512:(g + 1) * 512], in_=P[b][0:E, :]))
            evac_done("vector", b)

    def emit_ffn_experts(vstage, b3row):
        ar.push()
        w1h = ar.bf16(DCH * D)
        w1l = ar.bf16(DCH * D)
        w2h = ar.bf16(DCH * D)
        w2l = ar.bf16(DCH * D)
        w3h = ar.bf16(DCH * D)
        w3l = ar.bf16(DCH * D)
        wstage = [ar.f32(512) for _ in range(2)]
        lnh = [ar.bf16(DCH * TC)]
        lnl = [ar.bf16(DCH * TC)]
        h1h = ar.bf16(DCH * TC)
        h1l = ar.bf16(DCH * TC)
        h2h = ar.bf16(DCH * TC)
        h2l = ar.bf16(DCH * TC)
        ev32 = [ar.f32(TC) for _ in range(2)]
        wy = [ar.f32(TC) for _ in range(2)]

        wmats = {0: (W1, w1h, w1l), 1: (W2, w2h, w2l), 2: (W3, w3h, w3l)}
        stage_free = [(None, 0), (None, 0)]
        ln_free = [(None, 0)]
        vstage_free = [(None, 0)]
        ev_free = [(None, 0), (None, 0)]
        wy_free = [(None, 0), (None, 0)]
        pend_pv = [None]

        si_ctr = [0]

        def load_split(e_i, mi, free_st):
            WD, wh, wl = wmats[mi]
            plan.wait("gpsimd", ST, free_st)
            plan.wait("sync", ST, free_st)
            for kc in range(DCH):
                for q in range(2):
                    si = si_ctr[0] % 2
                    si_ctr[0] += 1
                    c, v = stage_free[si]
                    if c is not None:
                        plan.wait("sync", c, v)
                    ld = plan.dma("sync", SQ, wstage[si],
                                  WD[e_i, kc * 128:(kc + 1) * 128,
                                     q * 512:(q + 1) * 512])
                    co = kc * D + q * 512
                    plan.wait("gpsimd", SQ, ld)
                    plan.op("gpsimd", lambda e, si=si, wh=wh, co=co:
                            e.tensor_copy(out=wh[:, co:co + 512],
                                          in_=wstage[si]))
                    sf = plan.op_inc("gpsimd", SP, 1,
                                     lambda e, si=si, wh=wh, wl=wl, co=co:
                                     e.tensor_tensor(
                                         out=wl[:, co:co + 512],
                                         in0=wstage[si],
                                         in1=wh[:, co:co + 512],
                                         op=ALU.subtract))
                    stage_free[si] = (SP, sf)
            return SG.n, SP.n

        wsplit = [0, 0]
        pref_vals = [None, None]
        for e_i in range(E):
            prev_mm = ST.n
            if e_i == 0:
                for mi in range(3):
                    s_, v_ = load_split(0, mi, 0)
                wsplit = [s_, v_]
            wsplit_s, wsplit_v = wsplit
            plan.wait("sync", ST, prev_mm)
            ldb3 = plan.dma("sync", SQ, b3row, B3[e_i:e_i + 1, :])
            plan.wait("tensor", SP, wsplit_v)
            plan.wait("tensor", SG, wsplit_s)
            plan.wait("tensor", SQ, ldb3)

            def layer(wh, wl, xh, xl, outh, outl, bcol_of):
                for oc in range(DCH):
                    bk = oc % 2
                    mm_bank(bk)
                    for kc in range(DCH):
                        for p_i, (wa, xa) in enumerate(
                                ((wh, xh), (wh, xl), (wl, xh))):
                            plan.op_inc(
                                "tensor", ST, 1,
                                lambda e, wa=wa, xa=xa, oc=oc, kc=kc, bk=bk,
                                st=(kc == 0 and p_i == 0),
                                sp=(kc == DCH - 1 and p_i == 2): e.matmul(
                                    P[bk][:],
                                    wa[:, kc * D + oc * 128:
                                       kc * D + (oc + 1) * 128],
                                    xa[:, kc * TC:(kc + 1) * TC],
                                    start=st, stop=sp))
                    bank_done(bk)
                    evac_wait("scalar", bk)
                    ei = oc % 2
                    c, v = ev_free[ei]
                    if c is not None:
                        plan.wait("scalar", c, v)
                    bcol = bcol_of(oc)
                    plan.op_inc("scalar", SS, 1,
                                lambda e, bk=bk, ei=ei, bcol=bcol: e.activation(
                                    out=ev32[ei], in_=P[bk][:], func=AF.Relu,
                                    bias=bcol))
                    evac_done("scalar", bk)
                    plan.op_inc("scalar", SS, 1,
                                lambda e, ei=ei, oc=oc, outh=outh: e.activation(
                                    out=outh[:, oc * TC:(oc + 1) * TC],
                                    in_=ev32[ei], func=AF.Identity))
                    plan.wait("vector", SS, SS.n)
                    ef = plan.op_inc("vector", SV, 1,
                                     lambda e, ei=ei, oc=oc, outh=outh,
                                     outl=outl: e.tensor_tensor(
                                         out=outl[:, oc * TC:(oc + 1) * TC],
                                         in0=ev32[ei],
                                         in1=outh[:, oc * TC:(oc + 1) * TC],
                                         op=ALU.subtract))
                    ev_free[ei] = (SV, ef)

            for tc in range(NTC):
                si = 0
                c, v = ln_free[si]
                if c is not None:
                    plan.wait("sync", c, v)
                sl = slice(tc * TC, (tc + 1) * TC)
                plan.dma("sync", SQ, lnh[si].rearrange("p (c t) -> p c t", c=DCH),
                         AFH[:, :, sl].transpose([1, 0, 2]))
                lnld = plan.dma("sync", SQ,
                                lnl[si].rearrange("p (c t) -> p c t", c=DCH),
                                AFL[:, :, sl].transpose([1, 0, 2]))
                plan.wait("tensor", SQ, lnld)

                layer(w1h, w1l, lnh[si], lnl[si], h1h, h1l,
                      lambda oc: b1c[:, e_i * 8 + oc:e_i * 8 + oc + 1])
                ln_free[si] = (ST, ST.n)
                if tc == NTC - 1 and e_i + 1 < E:
                    w1_sv = load_split(e_i + 1, 0, ST.n)
                    pref_vals[0] = w1_sv
                plan.wait("tensor", SV, SV.n)
                layer(w2h, w2l, h1h, h1l, h2h, h2l,
                      lambda oc: b2c[:, e_i * 8 + oc:e_i * 8 + oc + 1])
                if tc == NTC - 1 and e_i + 1 < E:
                    pref_vals[1] = load_split(e_i + 1, 1, ST.n)
                plan.wait("tensor", SV, SV.n)
                for tt in range(4):
                    for oh in range(2):
                        bk = 4 + (tt * 2 + oh) % 2
                        mm_bank(bk)
                        if pend_pv[0] is not None:
                            pend_pv[0]()
                            pend_pv[0] = None
                        plan.op_inc("tensor", ST, 1,
                                    lambda e, bk=bk, oh=oh: e.matmul(
                                        P[bk][:], one1[:],
                                        b3row[:, oh * 512:(oh + 1) * 512],
                                        start=True, stop=False))
                        for kc in range(DCH):
                            for p_i, (xa, wa) in enumerate(
                                    ((h2h, w3h), (h2h, w3l), (h2l, w3h))):
                                plan.op_inc(
                                    "tensor", ST, 1,
                                    lambda e, xa=xa, wa=wa, kc=kc, tt=tt, oh=oh,
                                    bk=bk,
                                    sp=(kc == DCH - 1 and p_i == 2): e.matmul(
                                        P[bk][:],
                                        xa[:, kc * TC + tt * 128:
                                           kc * TC + (tt + 1) * 128],
                                        wa[:, kc * D + oh * 512:
                                           kc * D + (oh + 1) * 512],
                                        start=False, stop=sp))
                        bank_done(bk)
                        evac_wait("scalar", bk)
                        wi = (tt * 2 + oh) % 2
                        c, v = wy_free[wi]
                        if c is not None:
                            plan.wait("scalar", c, v)
                        gtile = tc * 4 + tt
                        plan.op_inc("scalar", SS, 1,
                                    lambda e, bk=bk, wi=wi, gtile=gtile, e_i=e_i:
                                    e.activation(
                                        out=wy[wi], in_=P[bk][:], func=AF.Relu,
                                        scale=wtk[:, gtile * 8 + e_i:
                                                  gtile * 8 + e_i + 1]))
                        evac_done("scalar", bk)
                        plan.wait("tensor", SS, SS.n)
                        cpv, vpv = pv_state[0]
                        if cpv is not None:
                            plan.wait("tensor", cpv, vpv)
                        wf = plan.op_inc(
                            "tensor", ST, 1,
                            lambda e, wi=wi, oh=oh,
                            st=(tc == 0 and tt == 0),
                            sp=(tc == NTC - 1 and tt == 3): e.matmul(
                                PV[:, oh * 512:(oh + 1) * 512], onec[:],
                                wy[wi], start=st, stop=sp))
                        wy_free[wi] = (ST, wf)
            if pend_pv[0] is not None:
                pend_pv[0]()
                pend_pv[0] = None
            v_mm_done = ST.n
            if e_i + 1 < E:
                s_, v_ = load_split(e_i + 1, 2, v_mm_done)
                wsplit = [s_, v_]
            plan.wait("vector", ST, v_mm_done)
            c, v = vstage_free[0]
            if c is not None:
                plan.wait("vector", c, v)
            plan.op_inc("vector", SV, 1,
                        lambda e: e.tensor_copy(out=vstage, in_=PV[:]))
            vst = SV.n
            pv_state[0] = (SV, vst)
            plan.wait("sync", SV, vst)
            vf = plan.dma("sync", SQ, CCIN.ap()[e_i:e_i + 1, :], vstage)
            vstage_free[0] = (SQ, vf)
        ar.pop()

    def emit_allreduce(vall):
        plan.wait("gpsimd", SQ, SQ.n)
        plan.op_inc("gpsimd", SC, 1, lambda e: e.collective_compute(
            "AllReduce", ALU.add, replica_groups=[list(range(NCORES))],
            ins=[CCIN.ap().opt()], outs=[CCOUT.ap().opt()]))
        plan.wait("gpsimd", SC, SC.n)
        plan.dma("gpsimd", SG, vall, CCOUT.ap())

    def emit_ffn_apply(src, dst, vall):
        ar.push()
        xres = [ar.f32(4 * D) for _ in range(2)]
        xo = [ar.f32(512) for _ in range(2)]
        xres_free = [(None, 0), (None, 0)]
        xo_free = [(None, 0), (None, 0)]
        plan.wait("tensor", SG, SG.n)
        plan.wait("vector", SG, SG.n)
        for tc in range(NTC):
            si = tc % 2
            c, v = xres_free[si]
            if c is not None:
                plan.wait("sync", c, v)
            sl = slice(tc * TC, (tc + 1) * TC)
            xl = plan.dma("sync", SQ, xres[si].rearrange("p (t d) -> p t d", t=4),
                          src[sl, :].rearrange("(t p) d -> t p d", p=128)
                          .transpose([1, 0, 2]))
            plan.wait("vector", SQ, xl)
            for tt in range(4):
                for oh in range(2):
                    bk = (tt * 2 + oh) % 2
                    mm_bank(bk)
                    g = tc * 4 + tt
                    plan.op_inc("tensor", ST, 1, lambda e, g=g, oh=oh, bk=bk:
                                e.matmul(P[bk][:],
                                         ind_em[:, g * 128:(g + 1) * 128],
                                         vall[:, oh * 512:(oh + 1) * 512],
                                         start=True, stop=True))
                    bank_done(bk)
                    evac_wait("vector", bk)
                    oi = (tt * 2 + oh) % 2
                    c, v = xo_free[oi]
                    if c is not None:
                        plan.wait("vector", c, v)
                    plan.op_inc("vector", SV, 1,
                                lambda e, bk=bk, oi=oi, tt=tt, oh=oh, si=si:
                                e.scalar_tensor_tensor(
                                    out=xo[oi], in0=P[bk][:], scalar=0.5,
                                    in1=xres[si][:, tt * D + oh * 512:
                                                 tt * D + (oh + 1) * 512],
                                    op0=ALU.mult, op1=ALU.add))
                    evac_done("vector", bk)
                    plan.wait("sync", SV, SV.n)
                    of = plan.dma("sync", SQ,
                                  dst[tc * TC + tt * 128:
                                      tc * TC + (tt + 1) * 128,
                                      oh * 512:(oh + 1) * 512], xo[oi])
                    xo_free[oi] = (SQ, of)
            xres_free[si] = (SV, SV.n)
        ar.pop()

    def emit_inproj():
        ar.push()
        wih = ar.bf16(DCH * 2 * DI)
        wil = ar.bf16(DCH * 2 * DI)
        wst = [ar.f32(512) for _ in range(2)]
        hh = [ar.bf16(DCH * TC) for _ in range(2)]
        hl = [ar.bf16(DCH * TC) for _ in range(2)]
        ev = [ar.f32(TC) for _ in range(2)]
        stage_free = [(None, 0), (None, 0)]
        h_free = [(None, 0), (None, 0)]
        ev_free = [(None, 0)] * 2
        for kc in range(DCH):
            for q in range(8):
                si = (kc * 8 + q) % 2
                c, v = stage_free[si]
                if c is not None:
                    plan.wait("sync", c, v)
                ld = plan.dma("sync", SQ, wst[si],
                              WIT[kc * 128:(kc + 1) * 128,
                                  q * 512:(q + 1) * 512])
                plan.wait("scalar", SQ, ld)
                co = kc * 2 * DI + q * 512
                plan.op_inc("scalar", SS, 1, lambda e, si=si, co=co:
                            e.activation(out=wih[:, co:co + 512],
                                         in_=wst[si], func=AF.Identity))
                plan.wait("vector", SS, SS.n)
                sf = plan.op_inc("vector", SV, 1, lambda e, si=si, co=co:
                                 e.tensor_tensor(
                                     out=wil[:, co:co + 512],
                                     in0=wst[si],
                                     in1=wih[:, co:co + 512],
                                     op=ALU.subtract))
                stage_free[si] = (SV, sf)
        plan.wait("tensor", SV, SV.n)
        plan.wait("tensor", SS, SS.n)
        for tc in range(NTC):
            si = tc % 2
            c, v = h_free[si]
            if c is not None:
                plan.wait("sync", c, v)
            sl = slice(tc * TC, (tc + 1) * TC)
            plan.dma("sync", SQ, hh[si].rearrange("p (c t) -> p c t", c=DCH),
                     AFH[:, :, sl].transpose([1, 0, 2]))
            hld = plan.dma("sync", SQ,
                           hl[si].rearrange("p (c t) -> p c t", c=DCH),
                           AFL[:, :, sl].transpose([1, 0, 2]))
            plan.wait("tensor", SQ, hld)
            for rc in range(32):
                bk = rc % 4
                mm_bank(bk)
                for kc in range(DCH):
                    for p_i, (wa, xa) in enumerate(
                            ((wih, hh[si]), (wih, hl[si]), (wil, hh[si]))):
                        plan.op_inc(
                            "tensor", ST, 1,
                            lambda e, wa=wa, xa=xa, kc=kc, rc=rc, bk=bk,
                            st=(kc == 0 and p_i == 0),
                            sp=(kc == DCH - 1 and p_i == 2): e.matmul(
                                P[bk][:],
                                wa[:, kc * 2 * DI + rc * 128:
                                   kc * 2 * DI + (rc + 1) * 128],
                                xa[:, kc * TC:(kc + 1) * TC],
                                start=st, stop=sp))
                bank_done(bk)
                ei = rc % 2
                c, v = ev_free[ei]
                if rc < CHT:
                    evac_wait("vector", bk)
                    if c is not None:
                        plan.wait("vector", c, v)
                    plan.op_inc("vector", SV, 1, lambda e, bk=bk, ei=ei:
                                e.tensor_copy(out=ev[ei], in_=P[bk][:]))
                    evac_done("vector", bk)
                    plan.wait("sync", SV, SV.n)
                    ef = plan.dma("sync", SQ, XZPRE[rc, :, sl], ev[ei])
                else:
                    evac_wait("scalar", bk)
                    if c is not None:
                        plan.wait("scalar", c, v)
                    plan.op_inc("scalar", SS, 1, lambda e, bk=bk, ei=ei:
                                e.activation(out=ev[ei], in_=P[bk][:],
                                             func=AF.Silu))
                    evac_done("scalar", bk)
                    plan.wait("sync", SS, SS.n)
                    ef = plan.dma("sync", SQ, SILUZ[rc - CHT, :, sl], ev[ei])
                ev_free[ei] = (SQ, ef)
            h_free[si] = (ST, ST.n)
        ar.pop()

    def emit_conv_proj():
        ar.push()
        xpt = ar.f32(CHT * 96)
        dpt = ar.f32(DI)
        ccarry = ar.f32(CHT * 3)
        cacc = ar.f32(TC)
        xip = [ar.f32(CHT * TC) for _ in range(2)]
        wrk = ar.f32(515)
        xia = ar.f32(CHT * TC)
        dtb64 = ar.f32(TC)
        ev = [ar.f32(TC) for _ in range(2)]
        xip_free = [(None, 0), (None, 0)]
        ev_free = [(None, 0), (None, 0)]
        xia_stored = 0
        plan.dma("sync", SQ, xpt.rearrange("p (c r) -> p c r", c=CHT),
                 XPT.rearrange("(c p) r -> c p r", p=128).transpose([1, 0, 2]))
        wld = plan.dma("sync", SQ, dpt[0:DTR, :], DPT)
        plan.wait("tensor", SQ, wld)
        for tc in range(NTC):
            si = tc % 2
            c, v = xip_free[si]
            if c is not None:
                plan.wait("sync", c, v)
            sl = slice(tc * TC, (tc + 1) * TC)
            xld = plan.dma("sync", SQ,
                           xip[si].rearrange("p (c t) -> p c t", c=CHT),
                           XZPRE[:, :, sl].transpose([1, 0, 2]))
            plan.wait("vector", SQ, xld)
            plan.wait("vector", SQ, xia_stored)
            for rc in range(CHT):
                xr = xip[si][:, rc * TC:(rc + 1) * TC]
                xo = xia[:, rc * TC:(rc + 1) * TC]
                if tc % 4 == 0:
                    plan.op("vector", lambda e, rc=rc: e.memset(
                        ccarry[:, rc * 3:(rc + 1) * 3], 0.0))
                plan.wait("vector", SS, SS.n)
                plan.op("vector", lambda e, rc=rc: e.tensor_copy(
                    out=wrk[:, 0:3], in_=ccarry[:, rc * 3:(rc + 1) * 3]))
                plan.op("vector", lambda e, xr=xr: e.tensor_copy(
                    out=wrk[:, 3:515], in_=xr))
                plan.op("vector", lambda e, rc=rc, xr=xr: e.tensor_copy(
                    out=ccarry[:, rc * 3:(rc + 1) * 3], in_=xr[:, 509:512]))
                plan.op("vector", lambda e, rc=rc: e.tensor_scalar(
                    out=cacc, in0=wrk[:, 3:515],
                    scalar1=cw[:, rc * 4 + 3:rc * 4 + 4], scalar2=None,
                    op0=ALU.mult))
                for j in range(3):
                    plan.op("vector",
                            lambda e, rc=rc, j=j: e.scalar_tensor_tensor(
                                out=cacc, in0=wrk[:, j:j + 512],
                                scalar=cw[:, rc * 4 + j:rc * 4 + j + 1],
                                in1=cacc, op0=ALU.mult, op1=ALU.add))
                plan.op_inc("vector", SV, 1, lambda e: e.drain())
                plan.wait("scalar", SV, SV.n)
                plan.op_inc("scalar", SS, 1, lambda e, rc=rc, xo=xo:
                            e.activation(out=xo, in_=cacc, func=AF.Silu,
                                         bias=cb[:, rc:rc + 1]))
            silu_done = SS.n
            plan.wait("sync", SS, silu_done)
            xia_stored = plan.dma("sync", SQ,
                                  XIACT[:, :, sl].transpose([1, 0, 2]),
                                  xia.rearrange("p (c t) -> p c t", c=CHT))
            xip_free[si] = (SV, SV.n)
            plan.wait("tensor", SS, silu_done)
            mm_bank(4)
            mm_bank(5)
            for rc in range(CHT):
                plan.op_inc("tensor", ST, 1, lambda e, rc=rc,
                            st=(rc == 0), sp=(rc == CHT - 1): e.matmul(
                                P[4][0:DTR, :],
                                xpt[:, rc * 96:rc * 96 + DTR],
                                xia[:, rc * TC:(rc + 1) * TC],
                                start=st, stop=sp))
            bank_done(4)
            for rc in range(CHT):
                plan.op_inc("tensor", ST, 1, lambda e, rc=rc,
                            st=(rc == 0), sp=(rc == CHT - 1): e.matmul(
                                P[5][0:32, :],
                                xpt[:, rc * 96 + DTR:rc * 96 + 96],
                                xia[:, rc * TC:(rc + 1) * TC],
                                start=st, stop=sp))
            bank_done(5)
            evac_wait("vector", 4)
            plan.op_inc("vector", SV, 1, lambda e: e.tensor_copy(
                out=dtb64[0:DTR, :], in_=P[4][0:DTR, :]))
            evac_done("vector", 4)
            dtev = SV.n
            evac_wait("vector", 5)
            plan.op_inc("vector", SV, 1, lambda e, sl=sl: e.tensor_copy(
                out=bc[:, sl], in_=P[5][0:32, :]))
            evac_done("vector", 5)
            plan.wait("tensor", SV, dtev)
            for oc in range(CHT):
                bk = oc % 4
                mm_bank(bk)
                plan.op_inc("tensor", ST, 1, lambda e, oc=oc, bk=bk: e.matmul(
                    P[bk][:], dpt[0:DTR, oc * 128:(oc + 1) * 128],
                    dtb64[0:DTR, :], start=True, stop=True))
                bank_done(bk)
                evac_wait("vector", bk)
                ei = oc % 2
                c, v = ev_free[ei]
                if c is not None:
                    plan.wait("vector", c, v)
                plan.op_inc("vector", SV, 1, lambda e, bk=bk, ei=ei:
                            e.tensor_copy(out=ev[ei], in_=P[bk][:]))
                evac_done("vector", bk)
                plan.wait("sync", SV, SV.n)
                ef = plan.dma("sync", SQ, DTPRE[oc, :, sl], ev[ei])
                ev_free[ei] = (SQ, ef)
        ar.pop()

    def emit_scan():
        ar.push()
        bt = ar.f32(DS * LC)
        ct = ar.f32(DS * LC)
        xi_t = [ar.f32(LC) for _ in range(2)]
        sz_t = [ar.f32(LC) for _ in range(2)]
        dt_t = [ar.f32(LC) for _ in range(2)]
        sp1 = ar.f32(LC)
        delta = ar.f32(LC)
        du = ar.f32(LC)
        da = [ar.f32(LC + 1), ar.f32(LC + 1)]
        db = ar.f32(LC + 1)
        sco = ar.f32(LC + 1)
        sco2 = ar.f32(LC + 1)
        tmp = ar.f32(LC)
        acc = ar.f32(LC)
        tmpg = ar.f32(LC)
        accg = ar.f32(LC)
        states = ar.f32(CHT * DS)
        yo = [ar.f32(LC) for _ in range(2)]
        st_free = [(None, 0), (None, 0)]
        yo_free = [(None, 0), (None, 0)]
        prev_merge = [0]
        da_used = [0, 0]
        plan.dma("sync", SQ, sel, SEL)
        plan.wait("tensor", SQ, SQ.n)
        plan.op("vector", lambda e: e.memset(da[0][:, 0:1], 0.0))
        plan.op("vector", lambda e: e.memset(da[1][:, 0:1], 0.0))
        prev_lc_v = 0
        for b in range(NB):
            plan.op_inc("vector", SV, 1, lambda e: e.memset(states, 0.0))
            for lc in range(NLC):
                t0 = b * L + lc * LC
                sl = slice(t0, t0 + LC)
                plan.wait("scalar", SV, prev_lc_v)
                for n in range(2 * DS):
                    bk = n % 2
                    mm_bank(bk)
                    plan.op_inc("tensor", ST, 1, lambda e, n=n, bk=bk, sl=sl:
                                e.matmul(P[bk][:],
                                         sel[:, n * 128:(n + 1) * 128],
                                         bc[:, sl], start=True, stop=True))
                    bank_done(bk)
                    evac_wait("scalar", bk)
                    dstv = (bt if n < DS else ct)
                    nn = n % DS
                    plan.op_inc("scalar", SS, 1, lambda e, bk=bk, dstv=dstv,
                                nn=nn: e.activation(
                                    out=dstv[:, nn * LC:(nn + 1) * LC],
                                    in_=P[bk][:], func=AF.Identity))
                    evac_done("scalar", bk)
                btct_done = SS.n
                for cht in range(CHT):
                    si = cht % 2
                    c, v = st_free[si]
                    if c is not None:
                        plan.wait("sync", c, v)
                    plan.dma("sync", SQ, xi_t[si], XIACT[cht, :, sl])
                    plan.dma("sync", SQ, sz_t[si], SILUZ[cht, :, sl])
                    ldd = plan.dma("sync", SQ, dt_t[si], DTPRE[cht, :, sl])
                    plan.wait("scalar", SQ, ldd)
                    plan.wait("scalar", SV, SV.n)
                    plan.op("scalar", lambda e, si=si, cht=cht: e.activation(
                        out=sp1, in_=dt_t[si], func=AF.Abs,
                        bias=dtb[:, cht:cht + 1]))
                    plan.op("scalar", lambda e: e.activation(
                        out=sp1, in_=sp1, func=AF.Exp, scale=-1.0))
                    plan.op("scalar", lambda e: e.activation(
                        out=sp1, in_=sp1, func=AF.Ln, bias=1.0))
                    plan.op_inc("scalar", SS, 1,
                                lambda e, si=si, cht=cht: e.activation(
                                    out=delta, in_=dt_t[si], func=AF.Relu,
                                    bias=dtb[:, cht:cht + 1]))
                    sp_done = SS.n
                    plan.wait("vector", SS, sp_done)
                    plan.op("vector", lambda e: e.tensor_tensor(
                        out=delta, in0=delta, in1=sp1, op=ALU.add))
                    plan.op_inc("vector", SV, 1,
                                lambda e, si=si: e.tensor_tensor(
                                    out=du, in0=delta, in1=xi_t[si],
                                    op=ALU.mult))
                    delta_done = SV.n
                    plan.wait("scalar", SV, delta_done)
                    plan.wait("vector", SS, btct_done)
                    sp_hist = []
                    for n in range(DS):
                        colidx = cht * DS + n
                        odd = (n % 2 == 1)
                        so = sco2 if odd else sco
                        par = n % 2
                        dap = da[par]
                        plan.wait("scalar", SV, da_used[par])
                        plan.op_inc("scalar", SS, 1,
                                    lambda e, colidx=colidx, dap=dap:
                                    e.activation(
                                        out=dap[:, 1:LC + 1], in_=delta,
                                        func=AF.Exp,
                                        scale=aneg[:, colidx:colidx + 1]))
                        da_done = SS.n
                        plan.op("vector", lambda e, colidx=colidx: e.tensor_copy(
                            out=db[:, 0:1], in_=states[:, colidx:colidx + 1]))
                        plan.op("vector", lambda e, n=n: e.tensor_tensor(
                            out=db[:, 1:LC + 1], in0=du,
                            in1=bt[:, n * LC:(n + 1) * LC], op=ALU.mult))
                        plan.wait("vector", SS, da_done)
                        if odd and len(sp_hist) >= 2:
                            plan.wait("vector", SP, sp_hist[-2])
                        if odd:
                            plan.op_inc("vector", SV, 1,
                                        lambda e, so=so, dap=dap:
                                        e.tensor_tensor_scan(
                                            out=so[:], data0=dap[:],
                                            data1=db[:], initial=0.0,
                                            op0=ALU.mult, op1=ALU.add))
                            sv_scan = SV.n
                            da_used[par] = SV.n
                        else:
                            plan.op_inc("vector", SV, 1,
                                        lambda e, so=so, dap=dap:
                                        e.tensor_tensor_scan(
                                            out=so[:], data0=dap[:],
                                            data1=db[:], initial=0.0,
                                            op0=ALU.mult, op1=ALU.add))
                            da_used[par] = SV.n
                        plan.op("vector",
                                lambda e, colidx=colidx, so=so: e.tensor_copy(
                                    out=states[:, colidx:colidx + 1],
                                    in_=so[:, LC:LC + 1]))
                        if odd:
                            plan.wait("gpsimd", SV, sv_scan)
                            if n == 1:
                                plan.wait("gpsimd", SV, prev_merge[0])
                                plan.op_inc("gpsimd", SP, 1,
                                            lambda e, n=n: e.tensor_tensor(
                                                out=accg,
                                                in0=sco2[:, 1:LC + 1],
                                                in1=ct[:, n * LC:(n + 1) * LC],
                                                op=ALU.mult))
                            else:
                                plan.op("gpsimd", lambda e, n=n: e.tensor_tensor(
                                    out=tmpg, in0=sco2[:, 1:LC + 1],
                                    in1=ct[:, n * LC:(n + 1) * LC],
                                    op=ALU.mult))
                                plan.op_inc("gpsimd", SP, 1,
                                            lambda e: e.tensor_tensor(
                                                out=accg, in0=accg, in1=tmpg,
                                                op=ALU.add))
                            sp_hist.append(SP.n)
                        else:
                            if n == 0:
                                plan.op_inc("vector", SV, 1,
                                            lambda e, n=n: e.tensor_tensor(
                                                out=acc, in0=sco[:, 1:LC + 1],
                                                in1=ct[:, n * LC:(n + 1) * LC],
                                                op=ALU.mult))
                            else:
                                plan.op("vector", lambda e, n=n: e.tensor_tensor(
                                    out=tmp, in0=sco[:, 1:LC + 1],
                                    in1=ct[:, n * LC:(n + 1) * LC], op=ALU.mult))
                                plan.op_inc("vector", SV, 1,
                                            lambda e: e.tensor_tensor(
                                                out=acc, in0=acc, in1=tmp,
                                                op=ALU.add))
                    plan.wait("vector", SP, sp_hist[-1])
                    plan.op_inc("vector", SV, 1, lambda e: e.tensor_tensor(
                        out=acc, in0=acc, in1=accg, op=ALU.add))
                    prev_merge[0] = SV.n
                    oi = cht % 2
                    c, v = yo_free[oi]
                    if c is not None:
                        plan.wait("vector", c, v)
                    plan.op("vector", lambda e, si=si, cht=cht:
                            e.scalar_tensor_tensor(
                                out=tmp, in0=xi_t[si],
                                scalar=ddi[:, cht:cht + 1], in1=acc,
                                op0=ALU.mult, op1=ALU.add))
                    plan.op_inc("vector", SV, 1, lambda e, si=si, oi=oi:
                                e.tensor_tensor(out=yo[oi], in0=tmp,
                                                in1=sz_t[si], op=ALU.mult))
                    st_free[si] = (SV, SV.n)
                    plan.wait("sync", SV, SV.n)
                    yf = plan.dma("sync", SQ, YCH[cht, :, sl], yo[oi])
                    yo_free[oi] = (SQ, yf)
                prev_lc_v = SV.n
        ar.pop()

    def emit_outproj():
        ar.push()
        woh = ar.bf16(CHT * D)
        wol = ar.bf16(CHT * D)
        wst = [ar.f32(D) for _ in range(2)]
        yh = ar.bf16(CHT * TC)
        yl = ar.bf16(CHT * TC)
        yf = [ar.f32(TC) for _ in range(2)]
        xres = ar.f32(4 * D)
        lres = ar.f32(4 * D)
        xo = [ar.f32(512) for _ in range(2)]
        stage_free = [(None, 0), (None, 0)]
        yf_free = [(None, 0), (None, 0)]
        xo_free = [(None, 0), (None, 0)]
        res_free = (None, 0)
        ysplit_t = 0
        for kc in range(CHT):
            si = kc % 2
            c, v = stage_free[si]
            if c is not None:
                plan.wait("sync", c, v)
            ld = plan.dma("sync", SQ, wst[si], WOT[kc * 128:(kc + 1) * 128, :])
            plan.wait("scalar", SQ, ld)
            plan.op_inc("scalar", SS, 1, lambda e, si=si, kc=kc: e.activation(
                out=woh[:, kc * D:(kc + 1) * D], in_=wst[si], func=AF.Identity))
            plan.wait("vector", SS, SS.n)
            sf = plan.op_inc("vector", SV, 1, lambda e, si=si, kc=kc:
                             e.tensor_tensor(out=wol[:, kc * D:(kc + 1) * D],
                                             in0=wst[si],
                                             in1=woh[:, kc * D:(kc + 1) * D],
                                             op=ALU.subtract))
            stage_free[si] = (SV, sf)
        plan.wait("tensor", SV, SV.n)
        plan.wait("tensor", SS, SS.n)
        for tc in range(NTC):
            sl = slice(tc * TC, (tc + 1) * TC)
            c, v = res_free
            if c is not None:
                plan.wait("sync", c, v)
            for cht in range(CHT):
                si = cht % 2
                c, v = yf_free[si]
                if c is not None:
                    plan.wait("sync", c, v)
                ld = plan.dma("sync", SQ, yf[si], YCH[cht, :, sl])
                plan.wait("scalar", SQ, ld)
                plan.wait("scalar", ST, ysplit_t)
                plan.wait("vector", ST, ysplit_t)
                plan.op_inc("scalar", SS, 1, lambda e, si=si, cht=cht:
                            e.activation(out=yh[:, cht * TC:(cht + 1) * TC],
                                         in_=yf[si], func=AF.Identity))
                plan.wait("vector", SS, SS.n)
                ff = plan.op_inc("vector", SV, 1, lambda e, si=si, cht=cht:
                                 e.tensor_tensor(
                                     out=yl[:, cht * TC:(cht + 1) * TC],
                                     in0=yf[si],
                                     in1=yh[:, cht * TC:(cht + 1) * TC],
                                     op=ALU.subtract))
                yf_free[si] = (SV, ff)
            ysplit = SV.n
            plan.dma("sync", SQ, xres.rearrange("p (t d) -> p t d", t=4),
                     X1[sl, :].rearrange("(t p) d -> t p d", p=128)
                     .transpose([1, 0, 2]))
            rl = plan.dma("sync", SQ, lres.rearrange("p (t d) -> p t d", t=4),
                          LNTM[sl, :].rearrange("(t p) d -> t p d", p=128)
                          .transpose([1, 0, 2]))
            plan.wait("tensor", SV, ysplit)
            plan.wait("vector", SQ, rl)
            for tt in range(4):
                for dh in range(2):
                    bk = 2 + (tt * 2 + dh) % 4
                    mm_bank(bk)
                    for kc in range(CHT):
                        for p_i, (xa, wa) in enumerate(
                                ((yh, woh), (yh, wol), (yl, woh))):
                            plan.op_inc(
                                "tensor", ST, 1,
                                lambda e, xa=xa, wa=wa, kc=kc, tt=tt, dh=dh,
                                bk=bk, st=(kc == 0 and p_i == 0),
                                sp=(kc == CHT - 1 and p_i == 2): e.matmul(
                                    P[bk][:],
                                    xa[:, kc * TC + tt * 128:
                                       kc * TC + (tt + 1) * 128],
                                    wa[:, kc * D + dh * 512:
                                       kc * D + (dh + 1) * 512],
                                    start=st, stop=sp))
                    bank_done(bk)
                    evac_wait("vector", bk)
                    oi = (tt * 2 + dh) % 2
                    c, v = xo_free[oi]
                    if c is not None:
                        plan.wait("vector", c, v)
                    plan.op("vector", lambda e, bk=bk, oi=oi, tt=tt, dh=dh:
                            e.tensor_tensor(
                                out=xo[oi], in0=P[bk][:],
                                in1=lres[:, tt * D + dh * 512:
                                         tt * D + (dh + 1) * 512],
                                op=ALU.add))
                    plan.op_inc("vector", SV, 1,
                                lambda e, oi=oi, tt=tt, dh=dh: e.tensor_tensor(
                                    out=xo[oi], in0=xo[oi],
                                    in1=xres[:, tt * D + dh * 512:
                                             tt * D + (dh + 1) * 512],
                                    op=ALU.add))
                    evac_done("vector", bk)
                    plan.wait("sync", SV, SV.n)
                    of = plan.dma("sync", SQ,
                                  X2[tc * TC + tt * 128:
                                     tc * TC + (tt + 1) * 128,
                                     dh * 512:(dh + 1) * 512], xo[oi])
                    xo_free[oi] = (SQ, of)
            ysplit_t = ST.n
            res_free = (SV, SV.n)
        ar.pop()

    def warmup(func):
        plan.op("scalar", lambda e: e.memzero(wcol[:, 0:1]))
        plan.op("scalar", lambda e: e.drain())
        plan.op("scalar", lambda e, func=func: e.activation(
            out=wcol[:, 1:2], in_=wcol[:, 0:1], func=func))
        plan.op("scalar", lambda e: e.nop(cycle_cnt=8000))

    def dbg_dump(vall):
        if not _DEBUG:
            return
        plan.wait("gpsimd", SV, SV.n)
        plan.wait("gpsimd", SS, SS.n)
        plan.wait("gpsimd", ST, ST.n)
        plan.dma("gpsimd", SG, DBG_SLAB, slab[:, :])
        plan.dma("gpsimd", SG, DBG_V, vall)
        plan.dma("gpsimd", SG, DBG_TM[:, 0:256], ptm[:])
        plan.dma("gpsimd", SG, DBG_TM[:, 256:512], indtm[:])
        plan.dma("gpsimd", SG, DBG_TM[:, 512:768], wtk[:])
        plan.wait("gpsimd", SG, SG.n)

    emit_const_loads()
    ar.push()
    vall = ar.f32(D)[0:E, :]
    vstage = ar.f32(D)[0:1, :]
    b3row = ar.f32(D)[0:1, :]
    emit_norm_phase(X, rms=False, gate=True, store_lntm=False, nw_scale=False)
    emit_gate_post()
    if upto >= 1:
        emit_ffn_experts(vstage, b3row)
        emit_allreduce(vall)
        emit_ffn_apply(X, X1, vall)
    dbg_dump(vall)
    ar.pop()
    barrier()
    with nc.named_scope("rb1_ffn1"):
        plan.flush(nc)
    if upto <= 1:
        return nc

    warmup(AF.Exp)
    emit_norm_phase(X1, rms=True, gate=False, store_lntm=True, nw_scale=True)
    barrier()
    with nc.named_scope("rb2_norm"):
        plan.flush(nc)
    if upto <= 2:
        return nc

    warmup(AF.Silu)
    emit_inproj()
    barrier()
    with nc.named_scope("rb3_inproj"):
        plan.flush(nc)
    if upto <= 3:
        return nc

    warmup(AF.Silu)
    emit_conv_proj()
    barrier()
    with nc.named_scope("rb4_convproj"):
        plan.flush(nc)
    if upto <= 4:
        return nc

    warmup(AF.Exp)
    emit_scan()
    barrier()
    with nc.named_scope("rb5_scan"):
        plan.flush(nc)
    if upto <= 5:
        return nc

    warmup(AF.Identity)
    emit_outproj()
    barrier()
    with nc.named_scope("rb6_outproj"):
        plan.flush(nc)
    if upto <= 6:
        return nc

    ar.push()
    vall = ar.f32(D)[0:E, :]
    vstage = ar.f32(D)[0:1, :]
    b3row = ar.f32(D)[0:1, :]
    warmup(AF.Exp)
    emit_norm_phase(X2, rms=False, gate=True, store_lntm=False, nw_scale=False)
    emit_gate_post()
    emit_ffn_experts(vstage, b3row)
    emit_allreduce(vall)
    emit_ffn_apply(X2, XOUT, vall)
    ar.pop()
    plan.wait("sync", SQ, SQ.n)
    barrier()
    with nc.named_scope("rb7_ffn2"):
        plan.flush(nc)

    return nc


def prep_inputs(inputs):
    x = np.ascontiguousarray(np.asarray(inputs["L0_attn_x"], np.float32))
    gw = np.ascontiguousarray(np.asarray(inputs["L0_ffn_gate_w"], np.float32))
    gb = np.asarray(inputs["L0_ffn_gate_b"], np.float32).reshape(E, 1)
    w1 = np.ascontiguousarray(np.asarray(inputs["L0_ffn_w1"], np.float32))
    w2 = np.ascontiguousarray(np.asarray(inputs["L0_ffn_w2"], np.float32))
    w3 = np.ascontiguousarray(np.asarray(inputs["L0_ffn_w3"], np.float32))
    b1 = np.ascontiguousarray(
        np.asarray(inputs["L0_ffn_b1"], np.float32).reshape(E, 8, 128)
        .transpose(0, 2, 1))
    b2 = np.ascontiguousarray(
        np.asarray(inputs["L0_ffn_b2"], np.float32).reshape(E, 8, 128)
        .transpose(0, 2, 1))
    b3 = np.ascontiguousarray(np.asarray(inputs["L0_ffn_b3"], np.float32))
    nw = np.ascontiguousarray(
        np.asarray(inputs["L0_attn_norm_w"], np.float32).reshape(DCH, 128).T)
    wit = np.ascontiguousarray(
        np.asarray(inputs["L0_attn_in_proj_w"], np.float32).T)
    cwh = np.ascontiguousarray(
        np.asarray(inputs["L0_attn_conv_w"], np.float32)
        .reshape(CHT, 128, 4).transpose(1, 0, 2))
    cbh = np.ascontiguousarray(
        np.asarray(inputs["L0_attn_conv_b"], np.float32).reshape(CHT, 128).T)
    xpt = np.ascontiguousarray(
        np.asarray(inputs["L0_attn_x_proj_w"], np.float32).T)
    dpt = np.ascontiguousarray(
        np.asarray(inputs["L0_attn_dt_proj_w"], np.float32).T)
    dtbh = np.ascontiguousarray(
        np.asarray(inputs["L0_attn_dt_proj_b"], np.float32).reshape(CHT, 128).T)
    alog = np.ascontiguousarray(
        np.asarray(inputs["L0_attn_A_log"], np.float32)
        .reshape(CHT, 128, DS).transpose(1, 0, 2).reshape(128, CHT * DS))
    ddi = np.ascontiguousarray(
        np.asarray(inputs["L0_attn_D"], np.float32).reshape(CHT, 128).T)
    wot = np.ascontiguousarray(
        np.asarray(inputs["L0_attn_out_proj_w"], np.float32).T)
    idt = np.eye(128, dtype=np.float32)
    one1 = np.ones((1, 128), dtype=np.float32)
    onec = np.ones((128, 1), dtype=np.float32)
    selm = np.ascontiguousarray(
        np.kron(np.eye(32, dtype=np.float32),
                np.ones((1, 128), dtype=np.float32)))
    shared = dict(GW=gw, GB=gb, W1=w1, W2=w2, W3=w3, B1=b1, B2=b2, B3=b3,
                  NW=nw, WIT=wit, CW=cwh, CB=cbh, XPT=xpt, DPT=dpt, DTB=dtbh,
                  ALOG=alog, DDI=ddi, WOT=wot, IDT=idt, ONE1=one1, ONEC=onec,
                  SEL=selm)
    in_maps = []
    for c in range(NCORES):
        m = dict(shared)
        m["X"] = np.ascontiguousarray(x[2 * c: 2 * c + 2].reshape(T, D))
        in_maps.append(m)
    return in_maps


def kernel(**inputs):
    in_maps = prep_inputs(inputs)
    nc = build_program()
    res = run_bass_kernel_spmd(nc, in_maps, core_ids=list(range(NCORES)))
    out = np.concatenate(
        [np.asarray(r["XOUT"]).reshape(NB, L, D) for r in res.results], axis=0)
    return out
